# version 18
# baseline (speedup 1.0000x reference)
import math
import numpy as np
import ml_dtypes
from contextlib import ExitStack

import concourse.bass as bass
import concourse.mybir as mybir
from concourse.bass_utils import run_bass_kernel_spmd

F32 = mybir.dt.float32
BF16 = mybir.dt.bfloat16
AF = mybir.ActivationFunctionType
ALU = mybir.AluOpType

B, S, E, H, G, D = 2, 4096, 2048, 16, 4, 128
INTER = 8192
T = 1024
LA, LB = 2048, 4096
NKA, NKB = LA // 128, LB // 128
NPA, NPB = NKA // 2, NKB // 2
CEXP = 30.0
INV_SQRT_D = 1.0 / math.sqrt(D)
NEG = -1.0e30
KC = E // 128
IC = INTER // 128


def hadamard(n):
    h = np.array([[1.0]])
    while h.shape[0] < n:
        h = np.block([[h, h], [h, -h]])
    return h


class Prog:
    def __init__(self, nc, stack):
        self.nc = nc
        self.stack = stack
        self.h = {}
        self.n = {}

    def sem(self, *names):
        for name in names:
            self.h[name] = self.stack.enter_context(self.nc.semaphore(name))
            self.n[name] = 0

    def inc(self, inst, name, k=1):
        inst.then_inc(self.h[name], k)
        self.n[name] += k
        return self.n[name]

    DMA_SEMS = ("qs", "qw", "xk", "wq", "ug", "wd", "d0", "d1", "d2", "d3",
                "d4", "d5", "d6", "d7", "e0", "e1", "e2", "e3", "e4", "e5",
                "e6", "e7")

    def w(self, eng, name, val=None):
        v = self.n[name] if val is None else val
        if val is not None and any(name.startswith(x) for x in self.DMA_SEMS):
            assert v == 0 or v == self.n[name], \
                f"partial DMA-sem wait {name} {v} != total {self.n[name]}"
        if v > 0:
            eng.wait_ge(self.h[name], v)

    def v(self, name):
        return self.n[name]


def build_nc(debug=False):
    nc = bass.Bass("TRN2", target_bir_lowering=False)
    dt = nc.dram_tensor

    xkvA = dt("xkvA", [E, LA], F32, kind="ExternalInput")
    xkvB = dt("xkvB", [E, LB], F32, kind="ExternalInput")
    xq = dt("xq", [E, T], F32, kind="ExternalInput")
    WqT = dt("WqT", [H, 128, KC * 128], F32, kind="ExternalInput")
    WkT = dt("WkT", [E, G * D], F32, kind="ExternalInput")
    WvT = dt("WvT", [E, G * D], F32, kind="ExternalInput")
    bq_col = dt("bq_col", [128, H], F32, kind="ExternalInput")
    bk_col = dt("bk_col", [128, G], F32, kind="ExternalInput")
    bv_row = dt("bv_row", [128, G * D], F32, kind="ExternalInput")
    WupT = dt("WupT", [IC, 128, KC * 128], F32, kind="ExternalInput")
    WgateT = dt("WgateT", [IC, 128, KC * 128], F32, kind="ExternalInput")
    WdownT = dt("WdownT", [KC, 128, IC * 128], F32, kind="ExternalInput")
    su_ug = dt("su_ug", [128, KC], F32, kind="ExternalInput")
    su_dp = dt("su_dp", [128, IC], F32, kind="ExternalInput")
    ws_up = dt("ws_up", [128, IC], F32, kind="ExternalInput")
    ws_gate = dt("ws_gate", [128, IC], F32, kind="ExternalInput")
    ws_dp = dt("ws_dp", [128, KC], F32, kind="ExternalInput")
    biasA = dt("biasA", [128, NPA], F32, kind="ExternalInput")
    biasB = dt("biasB", [128, NPB], F32, kind="ExternalInput")
    masks_in = dt("masks_in", [128, 2048], F32, kind="ExternalInput")
    h128_in = dt("h128_in", [128, 128], F32, kind="ExternalInput")
    st1_in = dt("st1_in", [128, 128], F32, kind="ExternalInput")
    st2_in = dt("st2_in", [128, 128], F32, kind="ExternalInput")

    out = dt("out", [KC, 128, T], F32, kind="ExternalOutput")

    scr_k = dt("scr_k", [G, 128, LA + LB], BF16, kind="Internal")
    scr_vt = dt("scr_vt", [48, 128, 512], BF16, kind="Internal")
    scr_f1 = dt("scr_f1", [KC, 128, T], BF16, kind="Internal")
    scr_f2 = dt("scr_f2", [IC, 128, T], BF16, kind="Internal")
    scr_xh = dt("scr_xh", [128, KC * T], F32, kind="Internal")
    scr_h2 = dt("scr_h2", [KC, 128, T], F32, kind="Internal")

    dbg = {}
    if debug:
        dbg["d_q"] = dt("d_q", [128, H * T], BF16, kind="ExternalOutput")
        dbg["d_h1"] = dt("d_h1", [128, KC * T], F32, kind="ExternalOutput")
        dbg["d_xh"] = dt("d_xh", [128, KC * T], F32, kind="ExternalOutput")
        dbg["d_t2"] = dt("d_t2", [128, KC * T], BF16, kind="ExternalOutput")
        dbg["d_bt"] = dt("d_bt", [128, IC * T], BF16, kind="ExternalOutput")

    with ExitStack() as stack:
        e = stack.enter_context
        p = Prog(nc, stack)
        p.sem("qs", "qw", "pe", "dv", "ac",
              "xk0", "xk1", "xk2", "wq0", "wq1", "ug0", "ug1", "wd0", "wd1",
              "d0", "d1", "d2", "d3", "d4", "d5", "d6", "d7", "pl", "ug2",
              "e0", "e1", "e2", "e3", "e4", "e5", "e6", "e7")

        ARF = e(nc.sbuf_tensor("ARF", [128, KC * T], F32))
        ARFb = ARF[:, :].bitcast(BF16)
        AR1 = e(nc.sbuf_tensor("AR1", [128, 26624], BF16))
        QSB = e(nc.sbuf_tensor("QSB", [128, H * T], BF16))
        WBUF = e(nc.sbuf_tensor("WBUF", [128, 4096], BF16))
        masks = e(nc.sbuf_tensor("masks", [128, 2048], BF16))
        h128 = e(nc.sbuf_tensor("h128", [128, 128], BF16))
        st1 = e(nc.sbuf_tensor("st1", [128, 128], BF16))
        st2 = e(nc.sbuf_tensor("st2", [128, 128], BF16))
        ones_b = e(nc.sbuf_tensor("ones_b", [128, 1], BF16))
        ones_fr = e(nc.sbuf_tensor("ones_fr", [1, 128], F32))
        ones_fc = e(nc.sbuf_tensor("ones_fc", [128, 1], F32))
        bqc = e(nc.sbuf_tensor("bqc", [128, H], F32))
        bkc = e(nc.sbuf_tensor("bkc", [128, G], F32))
        bvr = e(nc.sbuf_tensor("bvr", [128, G * D], F32))
        biasA_sb = e(nc.sbuf_tensor("biasA_sb", [128, NPA], F32))
        biasB_sb = e(nc.sbuf_tensor("biasB_sb", [128, NPB], F32))
        suug = e(nc.sbuf_tensor("suug", [128, KC], F32))
        sudp = e(nc.sbuf_tensor("sudp", [128, IC], F32))
        wsup = e(nc.sbuf_tensor("wsup", [128, IC], F32))
        wsgt = e(nc.sbuf_tensor("wsgt", [128, IC], F32))
        wsdp = e(nc.sbuf_tensor("wsdp", [128, KC], F32))
        scombo = e(nc.sbuf_tensor("scombo", [128, IC], F32))
        sgate = e(nc.sbuf_tensor("sgate", [128, IC], F32))
        sdpe = e(nc.sbuf_tensor("sdpe", [128, KC], F32))
        stg_b = e(nc.sbuf_tensor("stg_b", [128, 4096], BF16))
        stg_f = e(nc.sbuf_tensor("stg_f", [128, 4096], F32))
        rowbuf = e(nc.sbuf_tensor("rowbuf", [1, 4608], F32))
        cst = e(nc.sbuf_tensor("cst", [1, 8], F32))

        wk_sb = AR1[:, 0:KC * 512]
        wv_sb = AR1[:, KC * 512:2 * KC * 512]

        sy, gp, pe_, dv_, ac_ = nc.sync, nc.gpsimd, nc.tensor, nc.vector, nc.scalar
        ENGS = [pe_, dv_, ac_, sy, gp]
        SEMS = list(p.h.keys())

        def fence():
            for eng in ENGS:
                for s in SEMS:
                    p.w(eng, s)

        def mm(out_ap, lhsT, rhs, start, stop):
            return pe_.matmul(out_ap, lhsT, rhs, start=start, stop=stop,
                              skip_group_check=True)

        for dst, src in [(bqc, bq_col), (bkc, bk_col), (bvr, bv_row),
                         (biasA_sb, biasA), (biasB_sb, biasB),
                         (suug, su_ug), (sudp, su_dp), (wsup, ws_up),
                         (wsgt, ws_gate), (wsdp, ws_dp)]:
            p.inc(sy.dma_start(dst[:, :], src[:, :]), "qs", 16)
        for dst, src in [(masks, masks_in), (h128, h128_in), (st1, st1_in),
                         (st2, st2_in)]:
            p.inc(gp.dma_start(dst[:, :], src[:, :]), "qw", 16)
        p.inc(gp.dma_start(wk_sb[:, :], WkT.rearrange("(c p) n -> p c n", p=128)), "qw", 16)
        p.inc(gp.dma_start(wv_sb[:, :], WvT.rearrange("(c p) n -> p c n", p=128)), "qw", 16)
        qw_init = p.v("qw")

        p.w(dv_, "qs")
        dv_.memset(ones_b[:, :], 1.0)
        dv_.memset(cst[:, :], 1e-5)
        dv_.memset(ones_fr[:, :], 1.0)
        p.inc(dv_.memset(ones_fc[:, :], 1.0), "dv")
        C64R2 = 64.0 * math.sqrt(2.0)
        p.inc(dv_.tensor_scalar(sgate[:, :], wsgt[:, :], C64R2, None, ALU.mult), "dv")
        p.inc(dv_.tensor_scalar(scombo[:, :], wsup[:, :], C64R2, None, ALU.mult), "dv")
        p.w(dv_, "dv")
        p.inc(dv_.tensor_tensor(scombo[:, :], scombo[:, :], sudp[:, :], ALU.mult), "dv")
        p.inc(dv_.tensor_scalar(sdpe[:, :], wsdp[:, :], C64R2, None, ALU.mult), "dv")

        XS = KC * 512
        xslot = [ARFb[:, i * XS:(i + 1) * XS] for i in range(3)]
        xslot_done = [0, 0, 0]
        chunks = [("A", 0), ("B", 0), ("A", 1), ("A", 2), ("A", 3)] + \
                 [("B", j) for j in range(1, 8)]
        qw_at = [0] * len(chunks)
        kv_col = {"A": 0, "B": LA}

        def xkv_load(ci):
            stream, idx = chunks[ci]
            slot = ci % 3
            src = (xkvA if stream == "A" else xkvB)
            sv = src.rearrange("(c p) n -> p c n", p=128)[:, :, idx * 512:(idx + 1) * 512]
            p.w(gp, "pe", xslot_done[slot])
            qw_at[ci] = p.inc(gp.dma_start(
                xslot[slot].rearrange("p (c n) -> p c n", c=KC), sv), f"xk{slot}", 16)

        with nc.psum_tensor("ps_ring", [128, 2048], F32) as ps_ring_t:
            ps_ring = [ps_ring_t[:, i * 512:(i + 1) * 512] for i in range(4)]
            ps_free = [0, 0, 0, 0]
            ring_i = [0]

            def ring_next():
                i = ring_i[0] % 4
                ring_i[0] += 1
                return i

            stg = [stg_b[:, s * 512:(s + 1) * 512] for s in range(4)]
            stg_done = [0, 0, 0, 0]
            stg_i = [0]

            def evac_spill(pev, psd, bias_kind, bias_ap, dst_dram):
                si = stg_i[0] % 4
                stg_i[0] += 1
                p.w(dv_, "pe", pev)
                p.w(dv_, f"d{si}", stg_done[si])
                if bias_kind == "scalar":
                    dvv = p.inc(dv_.tensor_scalar(stg[si], psd, bias_ap, None, ALU.add), "dv")
                else:
                    dvv = p.inc(dv_.tensor_tensor(stg[si], psd, bias_ap, ALU.add), "dv")
                p.w(sy, "dv", dvv)
                stg_done[si] = p.inc(sy.dma_start(dst_dram, stg[si]), f"d{si}", 16)
                return dvv

            def kv_chunk(ci):
                stream, idx = chunks[ci]
                slot = ci % 3
                tcol = idx * 512
                p.w(pe_, "qw", qw_init)
                p.w(pe_, f"xk{slot}", qw_at[ci])
                for g in range(G):
                    pi = ring_next()
                    psd = ps_ring[pi]
                    p.w(pe_, "dv", ps_free[pi])
                    for kc in range(KC):
                        inst = mm(psd, wk_sb[:, kc * 512 + g * 128: kc * 512 + (g + 1) * 128],
                                  xslot[slot][:, kc * 512:(kc + 1) * 512],
                                  kc == 0, kc == KC - 1)
                    pev = p.inc(inst, "pe")
                    ps_free[pi] = evac_spill(
                        pev, psd, "scalar", bkc[:, g:g + 1],
                        scr_k[g][:, kv_col[stream] + tcol: kv_col[stream] + tcol + 512])
                for tt in range(4):
                    pi = ring_next()
                    psd = ps_ring[pi]
                    p.w(pe_, "dv", ps_free[pi])
                    for kc in range(KC):
                        inst = mm(psd,
                                  xslot[slot][:, kc * 512 + tt * 128: kc * 512 + (tt + 1) * 128],
                                  wv_sb[:, kc * 512:(kc + 1) * 512],
                                  kc == 0, kc == KC - 1)
                    pev = p.inc(inst, "pe")
                    vt_idx = (idx * 4 + tt) if stream == "A" else (16 + idx * 4 + tt)
                    ps_free[pi] = evac_spill(pev, psd, "tensor", bvr[:, :],
                                             scr_vt[vt_idx][:, :])
                xslot_done[slot] = p.v("pe")

            def q_proj():
                wq_done = [0, 0]
                wq_at = [0, 0]
                wqsl = [WBUF[:, s * 2048:(s + 1) * 2048] for s in range(2)]

                def wq_load(buf, dc):
                    p.w(gp, "pe", wq_done[buf])
                    wq_at[buf] = p.inc(gp.dma_start(
                        wqsl[buf], WqT[dc][:, :]), f"wq{buf}", 16)

                wq_load(0, 0)
                wq_load(1, 1)
                for dc in range(H):
                    buf = dc % 2
                    p.w(pe_, f"wq{buf}", wq_at[buf])
                    for half, slot in ((0, 0), (1, 1)):
                        pi = ring_next()
                        psd = ps_ring[pi]
                        p.w(pe_, "dv", ps_free[pi])
                        for kc in range(KC):
                            inst = mm(psd, wqsl[buf][:, kc * 128:(kc + 1) * 128],
                                      xslot[slot][:, kc * 512:(kc + 1) * 512],
                                      kc == 0, kc == KC - 1)
                        pev = p.inc(inst, "pe")
                        p.w(dv_, "pe", pev)
                        ps_free[pi] = p.inc(dv_.tensor_scalar(
                            QSB[:, dc * T + half * 512: dc * T + (half + 1) * 512],
                            psd, bqc[:, dc:dc + 1], None, ALU.add), "dv")
                    wq_done[buf] = p.v("pe")
                    if dc + 2 < H:
                        wq_load(buf, dc + 2)

            xkv_load(0)
            xkv_load(1)
            xkv_load(2)
            for ci in range(len(chunks)):
                kv_chunk(ci)
                if ci == 1:
                    q_proj()
                    xslot_done[0] = p.v("pe")
                    xslot_done[1] = p.v("pe")
                    xkv_load(3)
                    xkv_load(4)
                elif ci >= 2 and ci + 3 < len(chunks):
                    xkv_load(ci + 3)

        p.w(sy, "pe")
        xresid_qs = p.inc(sy.dma_start(
            ARF[:, :].rearrange("p (c n) -> p c n", c=KC),
            xq.rearrange("(c p) n -> p c n", p=128)), "d6", 16)

        if debug:
            p.w(sy, "dv")
            p.inc(sy.dma_start(dbg["d_q"][:, :], QSB[:, :]), "qs", 16)

        fence()

        kslot = [AR1[:, s * 12288: s * 12288 + 6144] for s in range(2)]
        vslot = [AR1[:, s * 12288 + 6144:(s + 1) * 12288] for s in range(2)]
        ebuf = [AR1[:, 24576 + i * 1024: 24576 + (i + 1) * 1024] for i in range(2)] \
            + [stg_b[:, 0:1024], stg_b[:, 1024:2048]]
        slot_load = [0, 0]
        slot_busy = [p.v("pe"), p.v("pe")]

        def kv_fetch(g):
            s = g % 2
            p.w(sy, "pe", slot_busy[s])
            for nm in ["d0", "d1", "d2", "d3"]:
                p.w(sy, nm)
            p.inc(sy.dma_start(kslot[s][:, :], scr_k[g][:, :]), f"d{4 + s}", 16)
            slot_load[s] = p.inc(sy.dma_start(
                vslot[s],
                scr_vt.rearrange("t p n -> p t n")[:, :, g * 128:(g + 1) * 128]),
                f"d{4 + s}", 16)

        kv_fetch(0)

        with (nc.psum_tensor("ps_s", [128, 2048], F32) as ps_s_t,
              nc.psum_tensor("ps_o", [128, 1024], F32) as ps_o_t,
              nc.psum_tensor("ps_rd", [128, 512], F32) as ps_rd,
              nc.psum_tensor("ps_d", [1, 512], F32) as ps_d):
            ps_s = [ps_s_t[:, i * 1024:(i + 1) * 1024] for i in range(2)]
            ps_o = [ps_o_t[:, i * 512:(i + 1) * 512] for i in range(2)]
            ps_s_rd = [0, 0]
            e_done = [0, 0, 0, 0]
            o_rd = [0, 0]
            d_rd = [0]
            ln_ac = [0]
            bcv_pe = [0]
            rdfree = [0]
            rowD = rowbuf[:, 0:512]
            rowR = rowbuf[:, 512:1024]
            rdb = stg_f[:, 0:512]
            pslice = stg_f[:, 512:1024]
            pend2 = [None]

            def epi_part2(force=True):
                if pend2[0] is None:
                    return
                ob, hcol, eneg, gset = pend2[0]
                if not force and gpair[0] - gset < 6:
                    return
                pend2[0] = None
                p.w(pe_, "ac", eneg)
                p.w(pe_, "dv", rdfree[0])
                bcv = p.inc(pe_.matmul(ps_rd[:, :], ones_fr[:, :], rowR,
                                       start=True, stop=True,
                                       skip_group_check=True), "pe")
                bcv_pe[0] = bcv
                p.w(dv_, "pe", bcv)
                p.w(dv_, "dv")
                rdfree[0] = p.inc(dv_.tensor_copy(rdb, ps_rd[:, :]), "dv")
                p.w(dv_, "dv")
                dvv = p.inc(dv_.scalar_tensor_tensor(
                    pslice, ps_o[ob], 1.0, rdb, ALU.mult, ALU.mult), "dv")
                p.w(dv_, "d6", xresid_qs)
                p.w(dv_, "dv")
                dvv = p.inc(dv_.tensor_tensor(
                    ARF[:, hcol:hcol + 512], ARF[:, hcol:hcol + 512], pslice,
                    ALU.add), "dv")
                o_rd[ob] = dvv

            pend = []
            gpair = [0]

            def emit_dav(s, ob, vtbase, npair, h, stripe, pr, eb, nk, nv):
                p.w(pe_, nk, nv)
                if pr == 0:
                    p.w(pe_, "dv", d_rd[0])
                    p.w(pe_, "dv", o_rd[ob])
                for k2 in range(2):
                    kt = pr * 2 + k2
                    pe_.matmul(ps_d[:, :], ones_b[:, :],
                               ebuf[eb][:, k2 * 512:(k2 + 1) * 512],
                               start=(kt == 0), stop=(kt == npair * 2 - 1),
                               skip_group_check=True)
                    inst = pe_.matmul(
                        ps_o[ob],
                        vslot[s][:, (vtbase + kt) * 128:(vtbase + kt + 1) * 128],
                        ebuf[eb][:, k2 * 512:(k2 + 1) * 512],
                        start=(kt == 0), stop=(kt == npair * 2 - 1),
                        skip_group_check=True)
                e_done[eb] = p.inc(inst, "pe")
                if pr == npair - 1:
                    epi_part1(ob, h, stripe, e_done[eb])

            def epi_part1(ob, h, stripe, pe_all):
                p.w(dv_, "pe", pe_all)
                p.w(dv_, "ac", ln_ac[0])
                p.w(dv_, "dv")
                dcp = p.inc(dv_.tensor_copy(rowD, ps_d[:, :]), "dv")
                d_rd[0] = dcp
                p.w(ac_, "dv", dcp)
                ln_ac[0] = p.inc(ac_.activation(rowD, rowD, AF.Ln), "ac")
                p.w(ac_, "pe", bcv_pe[0])
                p.w(ac_, "ac")
                eneg = p.inc(ac_.activation(rowR, rowD, AF.Exp, scale=-1.0), "ac")
                hcol = h * T + (0 if stripe == "A" else 512)
                pend2[0] = (ob, hcol, eneg, gpair[0])

            def flush_pend():
                while pend:
                    emit_dav(*pend.pop(0))
                epi_part2()

            def head_stripe(g, h, stripe, ob):
                s = g % 2
                npair = NPA if stripe == "A" else NPB
                kbase = 0 if stripe == "A" else LA
                vtbase = 0 if stripe == "A" else 16
                qoff = h * T + (0 if stripe == "A" else 512)
                bias_sb = biasA_sb if stripe == "A" else biasB_sb
                p.w(pe_, f"d{4 + s}", slot_load[s])
                for pr in range(npair):
                    sb = pr % 2
                    eb = gpair[0] % 4
                    gpair[0] += 1
                    p.w(pe_, "ac", ps_s_rd[sb])
                    for k2 in range(2):
                        kt = pr * 2 + k2
                        inst = mm(ps_s[sb][:, k2 * 512:(k2 + 1) * 512],
                                  kslot[s][:, kbase + kt * 128: kbase + (kt + 1) * 128],
                                  QSB[:, qoff: qoff + 512], True, True)
                    pev = p.inc(inst, "pe")
                    if len(pend) >= 4:
                        emit_dav(*pend.pop(0))
                    p.w(ac_, "pe", pev)
                    p.w(ac_, "pe", e_done[eb])
                    acv = p.inc(ac_.activation(
                        ebuf[eb][:, :], ps_s[sb][:, :], AF.Exp,
                        bias=bias_sb[:, pr:pr + 1], scale=INV_SQRT_D), "ac")
                    ps_s_rd[sb] = acv
                    nk, nv = "ac", acv
                    if pr < 2:
                        p.w(dv_, "ac", acv)
                        nv = p.inc(dv_.tensor_tensor(
                            ebuf[eb][:, :], ebuf[eb][:, :],
                            masks[:, pr * 1024:(pr + 1) * 1024], ALU.mult), "dv")
                        nk = "dv"
                    pend.append((s, ob, vtbase, npair, h, stripe, pr, eb, nk, nv))
                    epi_part2(force=False)

            hs = 0
            for g in range(G):
                if g + 1 < G:
                    kv_fetch(g + 1)
                for h in range(4 * g, 4 * g + 4):
                    for stripe in ("A", "B"):
                        head_stripe(g, h, stripe, hs % 2)
                        hs += 1
                flush_pend()
                slot_busy[g % 2] = p.v("pe")
            flush_pend()

        if debug:
            p.w(sy, "dv")
            p.inc(sy.dma_start(dbg["d_h1"][:, :], ARF[:, :]), "qs", 16)

        fence()

        R = rowbuf
        mean, var, arow, brow = (R[:, 512:1536], R[:, 1536:2560],
                                 R[:, 2560:3584], R[:, 3584:4608])

        def ln_stats(chunk_fn, nchunks, ps_sum, ps_sq, sqsls, xbsls):
            sq_pe = [0, 0]
            for c in range(nchunks):
                ch = chunk_fn(c)
                sb = c % 2
                p.w(ac_, "pe", sq_pe[sb])
                xcv = p.inc(ac_.copy(xbsls[sb], ch), "ac")
                p.w(dv_, "pe", sq_pe[sb])
                p.w(dv_, "dv")
                dcv = p.inc(dv_.tensor_tensor(sqsls[sb], ch, ch, ALU.mult), "dv")
                p.w(pe_, "ac", xcv)
                for half in range(2):
                    inst = mm(ps_sum[:, half * 512:(half + 1) * 512], ones_b[:, :],
                              xbsls[sb][:, half * 512:(half + 1) * 512],
                              c == 0, c == nchunks - 1)
                p.inc(inst, "pe")
                p.w(pe_, "dv", dcv)
                for half in range(2):
                    inst = mm(ps_sq[:, half * 512:(half + 1) * 512], ones_b[:, :],
                              sqsls[sb][:, half * 512:(half + 1) * 512],
                              c == 0, c == nchunks - 1)
                sq_pe[sb] = p.inc(inst, "pe")

        def ln_rows(ps_sum, ps_sq):
            p.w(ac_, "pe")
            p.inc(ac_.mul(mean, ps_sum[:, :], 1.0 / E), "ac")
            p.w(dv_, "ac")
            p.inc(dv_.tensor_tensor(var, mean, mean, ALU.mult), "dv")
            p.w(dv_, "dv")
            p.inc(dv_.scalar_tensor_tensor(var, ps_sq[:, :], 1.0 / E, var,
                                           ALU.mult, ALU.subtract), "dv")
            p.w(ac_, "dv")
            p.inc(ac_.activation(var, var, AF.Sqrt, bias=cst[:, 0:1], scale=1.0), "ac")
            p.w(dv_, "ac")
            p.inc(dv_.reciprocal(arow, var), "dv")
            p.w(dv_, "dv")
            p.inc(dv_.tensor_scalar(brow, mean, -1.0, None, ALU.mult), "dv")
            p.w(dv_, "dv")
            d5 = p.inc(dv_.tensor_tensor(brow, brow, arow, ALU.mult), "dv")
            return d5

        def ln_bcast(ps_a, ps_b, d5):
            p.w(pe_, "dv", d5)
            for half in range(2):
                pe_.matmul(ps_a[:, half * 512:(half + 1) * 512], ones_fr[:, :],
                           arow[:, half * 512:(half + 1) * 512], start=True,
                           stop=True, skip_group_check=True)
                inst = pe_.matmul(ps_b[:, half * 512:(half + 1) * 512], ones_fr[:, :],
                                  brow[:, half * 512:(half + 1) * 512], start=True,
                                  stop=True, skip_group_check=True)
            return p.inc(inst, "pe")

        subf = AR1[:, 18432:18448]
        p.w(dv_, "dv")
        p.inc(dv_.tensor_scalar(subf, suug[:, :], 1.0, None, ALU.mult), "dv")
        for c in range(KC):
            p.w(dv_, "dv")
            p.inc(dv_.tensor_scalar(masks[:, c * 128:(c + 1) * 128],
                                    h128[:, :], suug[:, c:c + 1], None,
                                    ALU.mult), "dv")
        dv_hs = p.v("dv")

        with nc.psum_tensor("ps_st", [1, 2048], F32) as ps_st:
            ps_sum, ps_sq = ps_st[:, 0:1024], ps_st[:, 1024:2048]
            ln_stats(lambda c: ARF[:, c * T:(c + 1) * T], KC, ps_sum, ps_sq,
                     [stg_b[:, 0:1024], stg_b[:, 3072:4096]],
                     [stg_b[:, 1024:2048], stg_b[:, 2048:3072]])
            d5 = ln_rows(ps_sum, ps_sq)

        if True:
            with (nc.psum_tensor("ps_bc", [128, 2048], F32) as ps_bc,
                  nc.psum_tensor("ps_f", [128, 2048], F32) as ps_f_t):
                ps_a, ps_b = ps_bc[:, 0:1024], ps_bc[:, 1024:2048]
                ps_f = [ps_f_t[:, i * 1024:(i + 1) * 1024] for i in range(2)]

                p.w(pe_, "dv", dv_hs)
                p.w(pe_, "dv", d5)
                p.w(pe_, "ac")
                for c in range(KC):
                    inst = pe_.matmul(ps_bc[0:1, c * 128:(c + 1) * 128],
                                      subf[:, c:c + 1], h128[:, :],
                                      start=True, stop=True,
                                      skip_group_check=True)
                pe_hs = p.inc(inst, "pe")
                hsut = AR1[0:1, 16384:18432]
                p.w(dv_, "pe", pe_hs)
                dv_hsut = p.inc(dv_.tensor_copy(hsut, ps_bc[0:1, 0:2048]), "dv")
                p.w(pe_, "dv", d5)
                p.w(pe_, "dv", dv_hsut)
                for half in range(2):
                    inst = pe_.matmul(ps_a[:, half * 512:(half + 1) * 512],
                                      ones_fr[:, :],
                                      arow[:, half * 512:(half + 1) * 512],
                                      start=True, stop=True,
                                      skip_group_check=True)
                pe_bc = p.inc(inst, "pe")
                browb = rowbuf[:, 0:512].bitcast(BF16)
                p.w(dv_, "dv")
                dv_bb = p.inc(dv_.tensor_copy(browb, brow), "dv")

                tsl = [stg_b[:, 1024:2048], stg_b[:, 2048:3072]]
                f1sl = [stg_b[:, 0:1024], stg_b[:, 3072:4096]]
                tsl_done = [0, 0]
                f1_done = [0, 0]
                f1_evac = [0, 0]
                t2 = AR1[:, 0:KC * T]
                f1r = scr_f1.rearrange("c (h j) t -> h c (j t)", h=8)
                rd_q = [(0, 0, 0)] * 4
                GPS = ["wq0", "wq1", "xk0", "xk1"]

                def bounce_read(gi):
                    j0, cnt = gi * 4, 4
                    dst = t2[:, j0 * T:(j0 + cnt) * T]
                    src = f1r[:, :, j0 * T:(j0 + cnt) * T]
                    p.w(gp, "d0")
                    p.w(gp, "d1")
                    p.w(sy, "d0")
                    p.w(sy, "d1")
                    p.w(ac_, "d0")
                    p.w(ac_, "d1")
                    a = p.inc(gp.dma_start(dst[0:64, :], src[0:4]), GPS[gi], 16)
                    bq_ = p.inc(sy.dma_start(dst[64:96, :], src[4:6]), f"e{gi}", 16)
                    cq = p.inc(ac_.dma_start(dst[96:128, :], src[6:8]),
                               f"e{4 + gi}", 16)
                    rd_q[gi] = (a, bq_, cq)

                p.w(dv_, "pe", pe_bc)
                p.w(ac_, "ac")
                p.w(ac_, "dv", dv_bb)
                for c in range(KC):
                    ch = ARF[:, c * T:(c + 1) * T]
                    sb = c % 2
                    p.w(dv_, "dv")
                    dvt = p.inc(dv_.tensor_tensor(ch, ch, ps_a, ALU.mult), "dv")
                    p.w(ac_, "dv", dvt)
                    p.w(ac_, "pe", tsl_done[sb])
                    acv = p.inc(ac_.copy(tsl[sb], ch), "ac")
                    p.w(pe_, "ac", acv)
                    p.w(pe_, "dv", f1_evac[sb])
                    for half in range(2):
                        mm(ps_f[sb][:, half * 512:(half + 1) * 512],
                           masks[:, c * 128:(c + 1) * 128],
                           tsl[sb][:, half * 512:(half + 1) * 512], True, False)
                        inst = pe_.matmul(
                            ps_f[sb][:, half * 512:(half + 1) * 512],
                            hsut[:, c * 128:(c + 1) * 128],
                            browb[:, half * 512:(half + 1) * 512],
                            start=False, stop=True, skip_group_check=True)
                    pev = p.inc(inst, "pe")
                    tsl_done[sb] = pev
                    p.w(dv_, "pe", pev)
                    p.w(dv_, f"d{sb}", f1_done[sb])
                    dvv = p.inc(dv_.tensor_copy(f1sl[sb], ps_f[sb][:, :]), "dv")
                    f1_evac[sb] = dvv
                    p.w(sy, "dv", dvv)
                    f1_done[sb] = p.inc(sy.dma_start(scr_f1[c][:, :], f1sl[sb]), f"d{sb}", 16)

                for gi in range(4):
                    bounce_read(gi)

                p.w(sy, "dv")
                p.inc(sy.dma_start(scr_xh[:, :], ARF[:, :]), "d7", 16)
                if debug:
                    p.inc(sy.dma_start(dbg["d_xh"][:, :], ARF[:, :]), "qs", 16)

                st2_evac = [0, 0]
                t2_ready = [0] * KC
                groups = [(0, 4), (4, 4), (8, 4), (12, 4)]
                for gi, (j0, cnt) in enumerate(groups):
                    p.w(pe_, GPS[gi], rd_q[gi][0])
                    p.w(pe_, f"e{gi}", rd_q[gi][1])
                    p.w(pe_, f"e{4 + gi}", rd_q[gi][2])
                    for j in range(j0, j0 + cnt):
                        sb = j % 2
                        p.w(pe_, "dv", st2_evac[sb])
                        for half in range(2):
                            inst = mm(ps_f[sb][:, half * 512:(half + 1) * 512],
                                      st1[:, :],
                                      t2[:, j * T + half * 512: j * T + (half + 1) * 512],
                                      True, True)
                        pev = p.inc(inst, "pe")
                        p.w(dv_, "pe", pev)
                        st2_evac[sb] = p.inc(
                            dv_.tensor_copy(t2[:, j * T:(j + 1) * T],
                                            ps_f[sb][:, :]), "dv")
                        t2_ready[j] = st2_evac[sb]

        if debug:
            p.w(sy, "dv")
            p.inc(sy.dma_start(dbg["d_t2"][:, :], t2[:, :]), "qs", 16)
            fence()

        st2_last = p.v("dv")

        wup_ring = [WBUF[:, 0:2048], QSB[:, 0:2048], QSB[:, 8192:10240]]
        wgt_ring = [WBUF[:, 2048:4096], QSB[:, 2048:4096], QSB[:, 10240:12288]]
        a_ring = [QSB[:, 4096 + s * 1024: 4096 + (s + 1) * 1024] for s in range(2)]
        b1_ring = [QSB[:, 6144 + s * 1024: 6144 + (s + 1) * 1024] for s in range(2)]
        gsl_ring = [stg_b[:, s * 1024:(s + 1) * 1024] for s in range(2)]
        wld = [0, 0, 0]
        wfree = [0, 0, 0]

        def wupgate_load(buf, c):
            p.w(gp, "pe", wfree[buf])
            sem = ["ug0", "ug1", "ug2"][buf]
            p.inc(gp.dma_start(wup_ring[buf], WupT[c][:, :]), sem, 16)
            wld[buf] = p.inc(gp.dma_start(wgt_ring[buf], WgateT[c][:, :]), sem, 16)

        with (nc.psum_tensor("ps_ug_a", [128, 2048], F32) as ps_ug_a,
              nc.psum_tensor("ps_ug_b", [128, 2048], F32) as ps_ug_b):
            PS = [ps_ug_a[:, 0:1024], ps_ug_b[:, 0:1024],
                  ps_ug_a[:, 1024:2048], ps_ug_b[:, 1024:2048]]
            pu_free = [0, 0]
            pg_free = [0, 0]
            a_done = [0, 0]
            gsl_done = [0, 0]
            b1_qs = [0, 0]
            pend = []

            def emit_stage1(c, buf, dvv, acv):
                pu = PS[2 * buf]
                p.w(pe_, "dv", dvv)
                p.w(pe_, "ac", acv)
                for half in range(2):
                    inst = mm(pu[:, half * 512:(half + 1) * 512], h128[:, :],
                              a_ring[buf][:, half * 512:(half + 1) * 512], True, True)
                pev = p.inc(inst, "pe")
                a_done[buf] = pev
                p.w(dv_, "pe", pev)
                p.w(dv_, f"d{buf}", b1_qs[buf])
                dvv = p.inc(dv_.tensor_copy(b1_ring[buf], pu[:, :]), "dv")
                pu_free[buf] = dvv
                p.w(sy, "dv", dvv)
                b1_qs[buf] = p.inc(sy.dma_start(scr_f2[c][:, :], b1_ring[buf]), f"d{buf}", 16)

            wupgate_load(0, 0)
            wupgate_load(1, 1)
            wupgate_load(2, 2)
            p.w(ac_, "d0")
            p.w(ac_, "d1")
            for c in range(IC):
                buf = c % 3
                pb = c % 2
                pu, pg = PS[2 * pb], PS[2 * pb + 1]
                p.w(pe_, ["ug0", "ug1", "ug2"][buf], wld[buf])
                p.w(pe_, "dv", pu_free[pb])
                if c == 1:
                    p.w(pe_, "dv", st2_last)
                for kc in range(KC):
                    if c == 0:
                        p.w(pe_, "dv", t2_ready[kc])
                    for half in range(2):
                        inst = mm(pu[:, half * 512:(half + 1) * 512],
                                  wup_ring[buf][:, kc * 128:(kc + 1) * 128],
                                  t2[:, kc * T + half * 512: kc * T + (half + 1) * 512],
                                  kc == 0, kc == KC - 1)
                p.inc(inst, "pe")
                if pend:
                    emit_stage1(*pend.pop(0))
                p.w(pe_, "ac", pg_free[pb])
                for kc in range(KC):
                    for half in range(2):
                        inst = mm(pg[:, half * 512:(half + 1) * 512],
                                  wgt_ring[buf][:, kc * 128:(kc + 1) * 128],
                                  t2[:, kc * T + half * 512: kc * T + (half + 1) * 512],
                                  kc == 0, kc == KC - 1)
                pe_gt = p.inc(inst, "pe")
                wfree[buf] = pe_gt
                if c + 3 < IC:
                    wupgate_load(buf, c + 3)
                p.w(ac_, "pe", pe_gt)
                p.w(ac_, "dv", gsl_done[pb])
                acv = p.inc(ac_.activation(gsl_ring[pb], pg[:, :], AF.Silu,
                                           bias=0.0, scale=sgate[:, c:c + 1]), "ac")
                pg_free[pb] = acv
                p.w(dv_, "ac", acv)
                p.w(dv_, "pe", a_done[pb])
                dvv = p.inc(dv_.scalar_tensor_tensor(
                    a_ring[pb], pu[:, :], scombo[:, c:c + 1], gsl_ring[pb],
                    ALU.mult, ALU.mult), "dv")
                gsl_done[pb] = dvv
                pend.append((c, pb, dvv, acv))
            while pend:
                emit_stage1(*pend.pop(0))

        fence()

        wd_ring = [QSB[:, s * 8192:(s + 1) * 8192] for s in range(2)]
        wd_ld = [0, 0]
        wd_free = [0, 0]

        def wd_load(buf, m):
            p.w(gp, "pe", wd_free[buf])
            wd_ld[buf] = p.inc(gp.dma_start(
                wd_ring[buf], WdownT[m][:, :]), f"wd{buf}", 16)


        def bt_chunk(j):
            if j < 32:
                return ARFb[:, j * T:(j + 1) * T]
            if j < 58:
                return AR1[:, (j - 32) * T:(j - 31) * T]
            if j < 62:
                return WBUF[:, (j - 58) * T:(j - 57) * T]
            return masks[:, (j - 62) * T:(j - 61) * T]

        f2r = scr_f2.rearrange("c (g j) t -> g c (j t)", g=2)
        with nc.psum_tensor("ps_s2", [128, 4096], F32) as ps_s2_t:
            ps_s2 = [ps_s2_t[:, i * 1024:(i + 1) * 1024] for i in range(4)]
            ev_done = [("dv", 0)] * 4
            rd_q = [(0, 0, 0)] * 4
            groups = ([(j, 4) for j in range(0, 56, 4)] +
                      [(56, 2), (58, 4), (62, 2)])

            def bt_dst(j0, cnt):
                if j0 < 32:
                    return ARFb[:, j0 * T:(j0 + cnt) * T]
                if j0 < 58:
                    return AR1[:, (j0 - 32) * T:(j0 - 32 + cnt) * T]
                if j0 < 62:
                    return WBUF[:, (j0 - 58) * T:(j0 - 58 + cnt) * T]
                return masks[:, (j0 - 62) * T:(j0 - 62 + cnt) * T]

            rd_pe4 = [0] * 4

            GPS = ["wq0", "wq1", "xk0", "xk1"]

            def f2read(gi):
                j0, cnt = groups[gi]
                sl = gi % 4
                dst = bt_dst(j0, cnt)
                src = f2r[:, :, j0 * T:(j0 + cnt) * T]
                p.w(gp, "pe", rd_pe4[sl])
                p.w(sy, "pe", rd_pe4[sl])
                p.w(ac_, "pe", rd_pe4[sl])
                a = p.inc(gp.dma_start(dst[0:64, :], src[0]), GPS[sl], 16)
                bq_ = p.inc(sy.dma_start(dst[64:96, :], src[1, 0:32]), f"d{sl}", 16)
                cq = p.inc(ac_.dma_start(dst[96:128, :], src[1, 32:64]),
                           f"d{4 + sl}", 16)
                rd_q[sl] = (a, bq_, cq)

            p.w(ac_, "ac")
            for gi in range(4):
                f2read(gi)
            p.w(gp, "pe")
            p.w(gp, "d0")
            p.w(gp, "d1")
            wd_load(0, 0)
            wd_load(1, 1)
            for gi, (j0, cnt) in enumerate(groups):
                sl = gi % 4
                p.w(pe_, GPS[sl], rd_q[sl][0])
                p.w(pe_, f"d{sl}", rd_q[sl][1])
                p.w(pe_, f"d{4 + sl}", rd_q[sl][2])
                last_pe = 0
                for j in range(j0, j0 + cnt):
                    pb = j % 4
                    p.w(pe_, *ev_done[pb])
                    for half in range(2):
                        inst = mm(ps_s2[pb][:, half * 512:(half + 1) * 512],
                                  st2[:, :],
                                  bt_chunk(j)[:, half * 512:(half + 1) * 512],
                                  True, True)
                    pev = p.inc(inst, "pe")
                    last_pe = pev
                    eng, sem = (dv_, "dv") if j % 2 == 0 else (ac_, "ac")
                    p.w(eng, "pe", pev)
                    if j % 2 == 0:
                        v = p.inc(eng.tensor_copy(bt_chunk(j), ps_s2[pb][:, :]), sem)
                    else:
                        v = p.inc(eng.copy(bt_chunk(j), ps_s2[pb][:, :]), sem)
                    ev_done[pb] = (sem, v)
                rd_pe4[sl] = last_pe
                if gi + 4 < len(groups):
                    f2read(gi + 4)

        if debug:
            p.w(sy, "dv")
            p.w(sy, "ac")
            for j in range(IC):
                p.inc(sy.dma_start(dbg["d_bt"][:, j * T:(j + 1) * T], bt_chunk(j)),
                      "qs", 16)

        fence()


        xh_ring = [stg_f[:, s * 1024:(s + 1) * 1024] for s in range(2)]
        xh_ld = [0, 0]
        xh_free = [0, 0]

        def xh_load(buf, m):
            p.w(sy, "dv", xh_free[buf])
            xh_ld[buf] = p.inc(sy.dma_start(
                xh_ring[buf], scr_xh[:, m * T:(m + 1) * T]), f"d{4 + buf}", 16)

        with (nc.psum_tensor("ps_y", [128, 2048], F32) as ps_y_t,
              nc.psum_tensor("ps_st2", [1, 2048], F32) as ps_st2):
            ps_y = [ps_y_t[:, i * 1024:(i + 1) * 1024] for i in range(2)]
            ps_sum2, ps_sq2 = ps_st2[:, 0:1024], ps_st2[:, 1024:2048]
            y_done = [0, 0]
            h2sl = [stg_f[:, 2048 + s * 1024: 2048 + (s + 1) * 1024] for s in range(2)]
            h2_state = [[0, 0, 0] for _ in range(2)]
            sq_pe = [0]
            sq_pe2 = [0]
            pend = [None]
            h2_qs_last = [0]

            def emit_stats(m, buf, dvv):
                p.w(sy, "dv", dvv)
                q = p.inc(sy.dma_start(scr_h2[m][:, :], h2sl[buf]), f"d{buf}", 16)
                h2_qs_last[0] = q
                sqsl = stg_b[:, buf * 1024:(buf + 1) * 1024]
                xbsl = stg_b[:, (2 + buf) * 1024:(3 + buf) * 1024]
                p.w(ac_, "pe", sq_pe[0] if buf == 0 else sq_pe2[0])
                p.w(ac_, "dv", dvv)
                p.inc(ac_.copy(xbsl, h2sl[buf]), "ac")
                acv = p.inc(ac_.activation(sqsl, h2sl[buf], AF.Square), "ac")
                p.w(pe_, "ac", acv)
                for half in range(2):
                    inst = mm(ps_sum2[:, half * 512:(half + 1) * 512], ones_b[:, :],
                              xbsl[:, half * 512:(half + 1) * 512],
                              m == 0, m == KC - 1)
                stat_pe = p.inc(inst, "pe")
                for half in range(2):
                    inst = mm(ps_sq2[:, half * 512:(half + 1) * 512], ones_b[:, :],
                              sqsl[:, half * 512:(half + 1) * 512],
                              m == 0, m == KC - 1)
                if buf == 0:
                    sq_pe[0] = p.inc(inst, "pe")
                else:
                    sq_pe2[0] = p.inc(inst, "pe")
                h2_state[buf] = [q, sq_pe[0], acv]

            xh_load(0, 0)
            xh_load(1, 1)
            for m in range(KC):
                buf = m % 2
                p.w(pe_, f"wd{buf}", wd_ld[buf])
                p.w(pe_, "dv", y_done[buf])
                for cc in range(IC):
                    for half in range(2):
                        inst = mm(ps_y[buf][:, half * 512:(half + 1) * 512],
                                  wd_ring[buf][:, cc * 128:(cc + 1) * 128],
                                  bt_chunk(cc)[:, half * 512:(half + 1) * 512],
                                  cc == 0, cc == IC - 1)
                pev = p.inc(inst, "pe")
                wd_free[buf] = pev
                if m + 2 < KC:
                    wd_load(buf, m + 2)
                if pend[0] is not None:
                    emit_stats(*pend[0])
                p.w(dv_, "pe", pev)
                p.w(dv_, f"d{4 + buf}", xh_ld[buf])
                st = h2_state[buf]
                p.w(dv_, f"d{buf}", st[0])
                p.w(dv_, "pe", st[1])
                p.w(dv_, "ac", st[2])
                dvv = p.inc(dv_.scalar_tensor_tensor(
                    h2sl[buf], ps_y[buf][:, :], sdpe[:, m:m + 1], xh_ring[buf],
                    ALU.mult, ALU.add), "dv")
                y_done[buf] = dvv
                xh_free[buf] = dvv
                if m + 2 < KC:
                    xh_load(buf, m + 2)
                pend[0] = (m, buf, dvv)
            emit_stats(*pend[0])

            p.w(sy, "dv")
            pre_h2 = [p.inc(sy.dma_start(stg_f[:, s * 1024:(s + 1) * 1024],
                                         scr_h2[s][:, :]), f"d{2 + s}", 16)
                      for s in range(2)]
            d5 = ln_rows(ps_sum2, ps_sq2)

        fence()

        with nc.psum_tensor("ps_bc2", [128, 2048], F32) as ps_bc2:
            ps_a2, ps_b2 = ps_bc2[:, 0:1024], ps_bc2[:, 1024:2048]
            pe_bc = ln_bcast(ps_a2, ps_b2, p.v("dv"))
            h2ring = [stg_f[:, s * 1024:(s + 1) * 1024] for s in range(2)]
            h2_rd = [0, 0]
            h2r_ld = [0, 0]

            def h2_fetch(buf, m):
                p.w(sy, "dv", h2_rd[buf])
                h2r_ld[buf] = p.inc(sy.dma_start(h2ring[buf], scr_h2[m][:, :]), f"d{2 + buf}", 16)

            ost = [stg_f[:, 2048:3072], stg_f[:, 3072:4096]]
            ost_qs = [0, 0]
            h2r_ld[0], h2r_ld[1] = pre_h2
            for m in range(KC):
                buf = m % 2
                p.w(dv_, f"d{2 + buf}", h2r_ld[buf])
                p.w(dv_, "pe", pe_bc)
                p.w(dv_, f"d{4 + buf}", ost_qs[buf])
                p.inc(dv_.tensor_tensor(ost[buf], h2ring[buf], ps_a2, ALU.mult), "dv")
                p.w(dv_, "dv")
                dvv = p.inc(dv_.tensor_tensor(ost[buf], ost[buf], ps_b2, ALU.add), "dv")
                h2_rd[buf] = dvv
                if m + 2 < KC:
                    h2_fetch(buf, m + 2)
                p.w(sy, "dv", dvv)
                ost_qs[buf] = p.inc(sy.dma_start(out[m][:, :], ost[buf]), f"d{4 + buf}", 16)

        for nm in p.h:
            if nm in ("pe", "dv", "ac"):
                continue
            p.w(sy if nm[0] in "dq" else gp, nm)

    return nc


def _prep_host(inputs):
    x = np.ascontiguousarray(np.asarray(inputs["L0_attn_x"]), dtype=np.float32)
    Wq = np.asarray(inputs["L0_attn_Wq"], np.float32)
    bq = np.asarray(inputs["L0_attn_bq"], np.float32)
    Wk = np.asarray(inputs["L0_attn_Wk"], np.float32)
    bk = np.asarray(inputs["L0_attn_bk"], np.float32)
    Wv = np.asarray(inputs["L0_attn_Wv"], np.float32)
    bv = np.asarray(inputs["L0_attn_bv"], np.float32)
    su_ug = np.asarray(inputs["L0_ffn_SU_ug"], np.float32)
    su_dp = np.asarray(inputs["L0_ffn_SU_dp"], np.float32)
    ws_ug = np.asarray(inputs["L0_ffn_Wscale_ug"], np.float32)
    ws_dp = np.asarray(inputs["L0_ffn_Wscale_dp"], np.float32)
    W_up = np.asarray(inputs["L0_ffn_W_up"], np.float32)
    W_gate = np.asarray(inputs["L0_ffn_W_gate"], np.float32)
    W_down = np.asarray(inputs["L0_ffn_W_down"], np.float32)

    xT = np.ascontiguousarray(x.transpose(0, 2, 1))
    WqT = np.ascontiguousarray(
        Wq.transpose(1, 0, 2).reshape(E, H * D)
        .reshape(16, 128, 16, 128).transpose(2, 1, 0, 3).reshape(H, 128, KC * 128))
    WkT = np.ascontiguousarray(Wk.transpose(1, 0, 2).reshape(E, G * D))
    WvT = np.ascontiguousarray(Wv.transpose(1, 0, 2).reshape(E, G * D))
    bq_col = np.ascontiguousarray(bq.T)
    bk_col = np.ascontiguousarray(bk.T)
    bv_row = np.ascontiguousarray(
        np.broadcast_to(bv.reshape(1, G * D), (128, G * D)))

    WupT = np.ascontiguousarray(
        W_up.T.reshape(16, 8, 16, INTER).transpose(2, 1, 0, 3).reshape(E, INTER)
        .reshape(16, 128, 64, 128).transpose(2, 1, 0, 3).reshape(IC, 128, KC * 128))
    WgateT = np.ascontiguousarray(
        W_gate.T.reshape(16, 8, 16, INTER).transpose(2, 1, 0, 3).reshape(E, INTER)
        .reshape(16, 128, 64, 128).transpose(2, 1, 0, 3).reshape(IC, 128, KC * 128))
    WdownT = np.ascontiguousarray(
        W_down.T.reshape(64, 2, 64, E).transpose(2, 1, 0, 3).reshape(INTER, E)
        .reshape(64, 128, 16, 128).transpose(2, 1, 0, 3).reshape(KC, 128, IC * 128))

    su_ug_c = np.ascontiguousarray(su_ug.reshape(KC, 128).T)
    su_dp_c = np.ascontiguousarray(su_dp.reshape(IC, 128).T)
    ws_up_c = np.ascontiguousarray(ws_ug[:INTER].reshape(IC, 128).T)
    ws_gt_c = np.ascontiguousarray(ws_ug[INTER:].reshape(IC, 128).T)
    ws_dp_c = np.ascontiguousarray(ws_dp.reshape(KC, 128).T)

    H128 = hadamard(128).astype(np.float32)
    ST1 = (np.kron(np.eye(8), hadamard(16)) * (2.0 ** -12)).astype(np.float32)
    ST2 = (np.kron(np.eye(2), hadamard(64)) * (2.0 ** -13)).astype(np.float32)
    masks = np.zeros((128, 2048), np.float32)
    for pr in range(2):
        for k2 in range(2):
            kt = pr * 2 + k2
            i = np.arange(128)[:, None]
            q = np.arange(512)[None, :]
            masks[:, pr * 1024 + k2 * 512: pr * 1024 + (k2 + 1) * 512] = \
                (kt * 128 + i <= q).astype(np.float32)

    shared = dict(WqT=WqT, WkT=WkT, WvT=WvT, bq_col=bq_col, bk_col=bk_col,
                  bv_row=bv_row, WupT=WupT, WgateT=WgateT, WdownT=WdownT,
                  su_ug=su_ug_c, su_dp=su_dp_c, ws_up=ws_up_c, ws_gate=ws_gt_c,
                  ws_dp=ws_dp_c, masks_in=masks, h128_in=H128, st1_in=ST1,
                  st2_in=ST2)

    in_maps = []
    for core in range(8):
        b, c = core // 4, core % 4
        oA, oB = 512 * c, 512 * (7 - c)
        klA, klB = oA + 512, oB + 512
        xkvA_ = np.zeros((E, LA), np.float32)
        xkvA_[:, 0:512] = xT[b][:, oA:oA + 512]
        xkvA_[:, 512:512 + oA] = xT[b][:, 0:oA]
        xkvB_ = np.zeros((E, LB), np.float32)
        xkvB_[:, 0:512] = xT[b][:, oB:oB + 512]
        xkvB_[:, 512:512 + oB] = xT[b][:, 0:oB]
        xq_ = np.ascontiguousarray(
            np.concatenate([xT[b][:, oA:oA + 512], xT[b][:, oB:oB + 512]], axis=1))
        biasA_ = np.full((128, NPA), -CEXP, np.float32)
        biasA_[:, klA // 256:] = NEG
        biasB_ = np.full((128, NPB), -CEXP, np.float32)
        biasB_[:, klB // 256:] = NEG
        m = dict(shared)
        m.update(xkvA=xkvA_, xkvB=xkvB_, xq=xq_, biasA=biasA_, biasB=biasB_)
        in_maps.append(m)
    return in_maps


def _assemble(results):
    outf = np.zeros((B, S, E), np.float32)
    for core in range(8):
        b, c = core // 4, core % 4
        oA, oB = 512 * c, 512 * (7 - c)
        sh = np.asarray(results[core]["out"], np.float32).reshape(E, T)
        outf[b, oA:oA + 512, :] = sh[:, 0:512].T
        outf[b, oB:oB + 512, :] = sh[:, 512:1024].T
    return outf


def kernel(**inputs):
    in_maps = _prep_host(inputs)
    nc = build_nc(debug=False)
    res = run_bass_kernel_spmd(nc, in_maps, core_ids=list(range(8)))
    return _assemble(res.results)


# version 19 (final)
# speedup vs baseline: 1.1723x; 1.1723x over previous
import math
import numpy as np
import ml_dtypes
from contextlib import ExitStack

import concourse.bass as bass
import concourse.mybir as mybir
from concourse.bass_utils import run_bass_kernel_spmd

F32 = mybir.dt.float32
BF16 = mybir.dt.bfloat16
AF = mybir.ActivationFunctionType
ALU = mybir.AluOpType

B, S, E, H, G, D = 2, 4096, 2048, 16, 4, 128
INTER = 8192
T = 1024
LA, LB = 2048, 4096
NKA, NKB = LA // 128, LB // 128
NPA, NPB = NKA // 2, NKB // 2
CEXP = 30.0
INV_SQRT_D = 1.0 / math.sqrt(D)
NEG = -1.0e30
KC = E // 128
IC = INTER // 128


def hadamard(n):
    h = np.array([[1.0]])
    while h.shape[0] < n:
        h = np.block([[h, h], [h, -h]])
    return h


class Prog:
    def __init__(self, nc, stack):
        self.nc = nc
        self.stack = stack
        self.h = {}
        self.n = {}

    def sem(self, *names):
        for name in names:
            self.h[name] = self.stack.enter_context(self.nc.semaphore(name))
            self.n[name] = 0

    def inc(self, inst, name, k=1):
        inst.then_inc(self.h[name], k)
        self.n[name] += k
        return self.n[name]

    DMA_SEMS = ("qs", "qw", "xk", "wq", "ug", "wd", "d0", "d1", "d2", "d3",
                "d4", "d5", "d6", "d7", "e0", "e1", "e2", "e3", "e4", "e5",
                "e6", "e7")

    def w(self, eng, name, val=None):
        v = self.n[name] if val is None else val
        if val is not None and any(name.startswith(x) for x in self.DMA_SEMS):
            assert v == 0 or v == self.n[name], \
                f"partial DMA-sem wait {name} {v} != total {self.n[name]}"
        if v > 0:
            eng.wait_ge(self.h[name], v)

    def v(self, name):
        return self.n[name]


def build_nc(debug=False):
    nc = bass.Bass("TRN2", target_bir_lowering=False)
    dt = nc.dram_tensor

    xkvA = dt("xkvA", [E, LA], F32, kind="ExternalInput")
    xkvB = dt("xkvB", [E, LB], F32, kind="ExternalInput")
    xq = dt("xq", [E, T], F32, kind="ExternalInput")
    WqT = dt("WqT", [H, 128, KC * 128], F32, kind="ExternalInput")
    WkT = dt("WkT", [E, G * D], F32, kind="ExternalInput")
    WvT = dt("WvT", [E, G * D], F32, kind="ExternalInput")
    bq_col = dt("bq_col", [128, H], F32, kind="ExternalInput")
    bk_col = dt("bk_col", [128, G], F32, kind="ExternalInput")
    bv_row = dt("bv_row", [128, G * D], F32, kind="ExternalInput")
    WupT = dt("WupT", [IC, 128, KC * 128], F32, kind="ExternalInput")
    WgateT = dt("WgateT", [IC, 128, KC * 128], F32, kind="ExternalInput")
    WdownT = dt("WdownT", [KC, 128, IC * 128], F32, kind="ExternalInput")
    su_ug = dt("su_ug", [128, KC], F32, kind="ExternalInput")
    su_dp = dt("su_dp", [128, IC], F32, kind="ExternalInput")
    ws_up = dt("ws_up", [128, IC], F32, kind="ExternalInput")
    ws_gate = dt("ws_gate", [128, IC], F32, kind="ExternalInput")
    ws_dp = dt("ws_dp", [128, KC], F32, kind="ExternalInput")
    biasA = dt("biasA", [128, NPA], F32, kind="ExternalInput")
    biasB = dt("biasB", [128, NPB], F32, kind="ExternalInput")
    masks_in = dt("masks_in", [128, 2048], F32, kind="ExternalInput")
    h128_in = dt("h128_in", [128, 128], F32, kind="ExternalInput")
    st1_in = dt("st1_in", [128, 128], F32, kind="ExternalInput")
    st2_in = dt("st2_in", [128, 128], F32, kind="ExternalInput")

    out = dt("out", [KC, 128, T], F32, kind="ExternalOutput")

    scr_k = dt("scr_k", [G, 128, LA + LB], BF16, kind="Internal")
    scr_vt = dt("scr_vt", [48, 128, 512], BF16, kind="Internal")
    scr_f1 = dt("scr_f1", [KC, 128, T], BF16, kind="Internal")
    scr_f2 = dt("scr_f2", [IC, 128, T], BF16, kind="Internal")
    scr_xh = dt("scr_xh", [128, KC * T], F32, kind="Internal")
    scr_h2 = dt("scr_h2", [KC, 128, T], F32, kind="Internal")

    dbg = {}
    if debug:
        dbg["d_q"] = dt("d_q", [128, H * T], BF16, kind="ExternalOutput")
        dbg["d_h1"] = dt("d_h1", [128, KC * T], F32, kind="ExternalOutput")
        dbg["d_xh"] = dt("d_xh", [128, KC * T], F32, kind="ExternalOutput")
        dbg["d_t2"] = dt("d_t2", [128, KC * T], BF16, kind="ExternalOutput")
        dbg["d_bt"] = dt("d_bt", [128, IC * T], BF16, kind="ExternalOutput")

    with ExitStack() as stack:
        e = stack.enter_context
        p = Prog(nc, stack)
        p.sem("qs", "qw", "pe", "dv", "ac",
              "xk0", "xk1", "xk2", "wq0", "wq1", "ug0", "ug1", "wd0", "wd1",
              "d0", "d1", "d2", "d3", "d4", "d5", "d6", "d7", "pl", "ug2",
              "e0", "e1", "e2", "e3", "e4", "e5", "e6", "e7")

        ARF = e(nc.sbuf_tensor("ARF", [128, KC * T], F32))
        ARFb = ARF[:, :].bitcast(BF16)
        AR1 = e(nc.sbuf_tensor("AR1", [128, 26624], BF16))
        QSB = e(nc.sbuf_tensor("QSB", [128, H * T], BF16))
        WBUF = e(nc.sbuf_tensor("WBUF", [128, 4096], BF16))
        masks = e(nc.sbuf_tensor("masks", [128, 2048], BF16))
        h128 = e(nc.sbuf_tensor("h128", [128, 128], BF16))
        st1 = e(nc.sbuf_tensor("st1", [128, 128], BF16))
        st2 = e(nc.sbuf_tensor("st2", [128, 128], BF16))
        ones_b = e(nc.sbuf_tensor("ones_b", [128, 1], BF16))
        ones_fr = e(nc.sbuf_tensor("ones_fr", [1, 128], F32))
        ones_fc = e(nc.sbuf_tensor("ones_fc", [128, 1], F32))
        bqc = e(nc.sbuf_tensor("bqc", [128, H], F32))
        bkc = e(nc.sbuf_tensor("bkc", [128, G], F32))
        bvr = e(nc.sbuf_tensor("bvr", [128, G * D], F32))
        biasA_sb = e(nc.sbuf_tensor("biasA_sb", [128, NPA], F32))
        biasB_sb = e(nc.sbuf_tensor("biasB_sb", [128, NPB], F32))
        suug = e(nc.sbuf_tensor("suug", [128, KC], F32))
        sudp = e(nc.sbuf_tensor("sudp", [128, IC], F32))
        wsup = e(nc.sbuf_tensor("wsup", [128, IC], F32))
        wsgt = e(nc.sbuf_tensor("wsgt", [128, IC], F32))
        wsdp = e(nc.sbuf_tensor("wsdp", [128, KC], F32))
        scombo = e(nc.sbuf_tensor("scombo", [128, IC], F32))
        sgate = e(nc.sbuf_tensor("sgate", [128, IC], F32))
        sdpe = e(nc.sbuf_tensor("sdpe", [128, KC], F32))
        stg_b = e(nc.sbuf_tensor("stg_b", [128, 4096], BF16))
        stg_f = e(nc.sbuf_tensor("stg_f", [128, 4096], F32))
        rowbuf = e(nc.sbuf_tensor("rowbuf", [1, 4608], F32))
        cst = e(nc.sbuf_tensor("cst", [1, 8], F32))

        wk_sb = AR1[:, 0:KC * 512]
        wv_sb = AR1[:, KC * 512:2 * KC * 512]

        sy, gp, pe_, dv_, ac_ = nc.sync, nc.gpsimd, nc.tensor, nc.vector, nc.scalar
        ENGS = [pe_, dv_, ac_, sy, gp]
        SEMS = list(p.h.keys())

        def fence():
            for eng in ENGS:
                for s in SEMS:
                    p.w(eng, s)

        def mm(out_ap, lhsT, rhs, start, stop):
            return pe_.matmul(out_ap, lhsT, rhs, start=start, stop=stop,
                              skip_group_check=True)

        for dst, src in [(bqc, bq_col), (bkc, bk_col), (bvr, bv_row),
                         (biasA_sb, biasA), (biasB_sb, biasB),
                         (suug, su_ug), (sudp, su_dp), (wsup, ws_up),
                         (wsgt, ws_gate), (wsdp, ws_dp)]:
            p.inc(sy.dma_start(dst[:, :], src[:, :]), "qs", 16)
        for dst, src in [(masks, masks_in), (h128, h128_in), (st1, st1_in),
                         (st2, st2_in)]:
            p.inc(gp.dma_start(dst[:, :], src[:, :]), "qw", 16)
        p.inc(gp.dma_start(wk_sb[:, :], WkT.rearrange("(c p) n -> p c n", p=128)), "qw", 16)
        p.inc(gp.dma_start(wv_sb[:, :], WvT.rearrange("(c p) n -> p c n", p=128)), "qw", 16)
        qw_init = p.v("qw")

        p.w(dv_, "qs")
        dv_.memset(ones_b[:, :], 1.0)
        dv_.memset(cst[:, :], 1e-5)
        dv_.memset(ones_fr[:, :], 1.0)
        p.inc(dv_.memset(ones_fc[:, :], 1.0), "dv")
        C64R2 = 64.0 * math.sqrt(2.0)
        p.inc(dv_.tensor_scalar(sgate[:, :], wsgt[:, :], C64R2, None, ALU.mult), "dv")
        p.inc(dv_.tensor_scalar(scombo[:, :], wsup[:, :], C64R2, None, ALU.mult), "dv")
        p.w(dv_, "dv")
        p.inc(dv_.tensor_tensor(scombo[:, :], scombo[:, :], sudp[:, :], ALU.mult), "dv")
        p.inc(dv_.tensor_scalar(sdpe[:, :], wsdp[:, :], C64R2, None, ALU.mult), "dv")

        XS = KC * 512
        xslot = [ARFb[:, i * XS:(i + 1) * XS] for i in range(3)]
        xslot_done = [0, 0, 0]
        chunks = [("A", 0), ("B", 0), ("A", 1), ("A", 2), ("A", 3)] + \
                 [("B", j) for j in range(1, 8)]
        qw_at = [0] * len(chunks)
        kv_col = {"A": 0, "B": LA}

        def xkv_load(ci):
            stream, idx = chunks[ci]
            slot = ci % 3
            src = (xkvA if stream == "A" else xkvB)
            sv = src.rearrange("(c p) n -> p c n", p=128)[:, :, idx * 512:(idx + 1) * 512]
            p.w(gp, "pe", xslot_done[slot])
            qw_at[ci] = p.inc(gp.dma_start(
                xslot[slot].rearrange("p (c n) -> p c n", c=KC), sv), f"xk{slot}", 16)

        with nc.psum_tensor("ps_ring", [128, 2048], F32) as ps_ring_t:
            ps_ring = [ps_ring_t[:, i * 512:(i + 1) * 512] for i in range(4)]
            ps_free = [0, 0, 0, 0]
            ring_i = [0]

            def ring_next():
                i = ring_i[0] % 4
                ring_i[0] += 1
                return i

            stg = [stg_b[:, s * 512:(s + 1) * 512] for s in range(4)]
            stg_done = [0, 0, 0, 0]
            stg_i = [0]

            def evac_spill(pev, psd, bias_kind, bias_ap, dst_dram):
                si = stg_i[0] % 4
                stg_i[0] += 1
                p.w(dv_, "pe", pev)
                p.w(dv_, f"d{si}", stg_done[si])
                if bias_kind == "scalar":
                    dvv = p.inc(dv_.tensor_scalar(stg[si], psd, bias_ap, None, ALU.add), "dv")
                else:
                    dvv = p.inc(dv_.tensor_tensor(stg[si], psd, bias_ap, ALU.add), "dv")
                p.w(sy, "dv", dvv)
                stg_done[si] = p.inc(sy.dma_start(dst_dram, stg[si]), f"d{si}", 16)
                return dvv

            def kv_chunk(ci):
                stream, idx = chunks[ci]
                slot = ci % 3
                tcol = idx * 512
                p.w(pe_, "qw", qw_init)
                p.w(pe_, f"xk{slot}", qw_at[ci])
                for g in range(G):
                    pi = ring_next()
                    psd = ps_ring[pi]
                    p.w(pe_, "dv", ps_free[pi])
                    for kc in range(KC):
                        inst = mm(psd, wk_sb[:, kc * 512 + g * 128: kc * 512 + (g + 1) * 128],
                                  xslot[slot][:, kc * 512:(kc + 1) * 512],
                                  kc == 0, kc == KC - 1)
                    pev = p.inc(inst, "pe")
                    ps_free[pi] = evac_spill(
                        pev, psd, "scalar", bkc[:, g:g + 1],
                        scr_k[g][:, kv_col[stream] + tcol: kv_col[stream] + tcol + 512])
                for tt in range(4):
                    pi = ring_next()
                    psd = ps_ring[pi]
                    p.w(pe_, "dv", ps_free[pi])
                    for kc in range(KC):
                        inst = mm(psd,
                                  xslot[slot][:, kc * 512 + tt * 128: kc * 512 + (tt + 1) * 128],
                                  wv_sb[:, kc * 512:(kc + 1) * 512],
                                  kc == 0, kc == KC - 1)
                    pev = p.inc(inst, "pe")
                    vt_idx = (idx * 4 + tt) if stream == "A" else (16 + idx * 4 + tt)
                    ps_free[pi] = evac_spill(pev, psd, "tensor", bvr[:, :],
                                             scr_vt[vt_idx][:, :])
                xslot_done[slot] = p.v("pe")

            def q_proj():
                wq_done = [0, 0]
                wq_at = [0, 0]
                wqsl = [WBUF[:, s * 2048:(s + 1) * 2048] for s in range(2)]

                def wq_load(buf, dc):
                    p.w(gp, "pe", wq_done[buf])
                    wq_at[buf] = p.inc(gp.dma_start(
                        wqsl[buf], WqT[dc][:, :]), f"wq{buf}", 16)

                wq_load(0, 0)
                wq_load(1, 1)
                for dc in range(H):
                    buf = dc % 2
                    p.w(pe_, f"wq{buf}", wq_at[buf])
                    for half, slot in ((0, 0), (1, 1)):
                        pi = ring_next()
                        psd = ps_ring[pi]
                        p.w(pe_, "dv", ps_free[pi])
                        for kc in range(KC):
                            inst = mm(psd, wqsl[buf][:, kc * 128:(kc + 1) * 128],
                                      xslot[slot][:, kc * 512:(kc + 1) * 512],
                                      kc == 0, kc == KC - 1)
                        pev = p.inc(inst, "pe")
                        p.w(dv_, "pe", pev)
                        ps_free[pi] = p.inc(dv_.tensor_scalar(
                            QSB[:, dc * T + half * 512: dc * T + (half + 1) * 512],
                            psd, bqc[:, dc:dc + 1], None, ALU.add), "dv")
                    wq_done[buf] = p.v("pe")
                    if dc + 2 < H:
                        wq_load(buf, dc + 2)

            xkv_load(0)
            xkv_load(1)
            xkv_load(2)
            for ci in range(len(chunks)):
                kv_chunk(ci)
                if ci == 1:
                    q_proj()
                    xslot_done[0] = p.v("pe")
                    xslot_done[1] = p.v("pe")
                    xkv_load(3)
                    xkv_load(4)
                elif ci >= 2 and ci + 3 < len(chunks):
                    xkv_load(ci + 3)

        p.w(sy, "pe")
        xresid_qs = p.inc(sy.dma_start(
            ARF[:, :].rearrange("p (c n) -> p c n", c=KC),
            xq.rearrange("(c p) n -> p c n", p=128)), "d6", 16)

        if debug:
            p.w(sy, "dv")
            p.inc(sy.dma_start(dbg["d_q"][:, :], QSB[:, :]), "qs", 16)

        fence()

        kslot = [AR1[:, s * 12288: s * 12288 + 6144] for s in range(2)]
        vslot = [AR1[:, s * 12288 + 6144:(s + 1) * 12288] for s in range(2)]
        ebuf = [AR1[:, 24576 + i * 1024: 24576 + (i + 1) * 1024] for i in range(2)] \
            + [stg_b[:, 0:1024], stg_b[:, 1024:2048]]
        slot_load = [(0, 0), (0, 0)]
        slot_busy = [p.v("pe"), p.v("pe")]

        def kv_fetch(g):
            s = g % 2
            vsrc = scr_vt.rearrange("t p n -> p t n")[:, :, g * 128:(g + 1) * 128]
            for eng in (sy, gp):
                p.w(eng, "pe", slot_busy[s])
                for nm in ["d0", "d1", "d2", "d3"]:
                    p.w(eng, nm)
            p.inc(gp.dma_start(kslot[s][0:64, :], scr_k[g][0:64, :]), f"xk{s}", 16)
            a2 = p.inc(gp.dma_start(vslot[s][0:64, :], vsrc[0:64]), f"xk{s}", 16)
            p.inc(sy.dma_start(kslot[s][64:128, :], scr_k[g][64:128, :]),
                  f"d{4 + s}", 16)
            b2 = p.inc(sy.dma_start(vslot[s][64:128, :], vsrc[64:128]), f"d{4 + s}", 16)
            slot_load[s] = (a2, b2)

        kv_fetch(0)

        with (nc.psum_tensor("ps_s", [128, 2048], F32) as ps_s_t,
              nc.psum_tensor("ps_o", [128, 1024], F32) as ps_o_t,
              nc.psum_tensor("ps_rd", [128, 512], F32) as ps_rd,
              nc.psum_tensor("ps_d", [1, 512], F32) as ps_d):
            ps_s = [ps_s_t[:, i * 1024:(i + 1) * 1024] for i in range(2)]
            ps_o = [ps_o_t[:, i * 512:(i + 1) * 512] for i in range(2)]
            ps_s_rd = [0, 0]
            e_done = [0, 0, 0, 0]
            o_rd = [0, 0]
            d_rd = [0]
            ln_ac = [0]
            bcv_pe = [0]
            rdfree = [0]
            rowD = rowbuf[:, 0:512]
            rowR = rowbuf[:, 512:1024]
            rdb = stg_f[:, 0:512]
            pslice = stg_f[:, 512:1024]
            pend2 = [None]

            def epi_part2(force=True):
                if pend2[0] is None:
                    return
                ob, hcol, eneg, gset = pend2[0]
                if not force and gpair[0] - gset < 6:
                    return
                pend2[0] = None
                p.w(pe_, "ac", eneg)
                p.w(pe_, "dv", rdfree[0])
                bcv = p.inc(pe_.matmul(ps_rd[:, :], ones_fr[:, :], rowR,
                                       start=True, stop=True,
                                       skip_group_check=True), "pe")
                bcv_pe[0] = bcv
                p.w(dv_, "pe", bcv)
                p.w(dv_, "dv")
                rdfree[0] = p.inc(dv_.tensor_copy(rdb, ps_rd[:, :]), "dv")
                p.w(dv_, "dv")
                dvv = p.inc(dv_.scalar_tensor_tensor(
                    pslice, ps_o[ob], 1.0, rdb, ALU.mult, ALU.mult), "dv")
                p.w(dv_, "d6", xresid_qs)
                p.w(dv_, "dv")
                dvv = p.inc(dv_.tensor_tensor(
                    ARF[:, hcol:hcol + 512], ARF[:, hcol:hcol + 512], pslice,
                    ALU.add), "dv")
                o_rd[ob] = dvv

            pend = []
            gpair = [0]

            def emit_dav(s, ob, vtbase, npair, h, stripe, pr, eb, nk, nv):
                p.w(pe_, nk, nv)
                if pr == 0:
                    p.w(pe_, "dv", d_rd[0])
                    p.w(pe_, "dv", o_rd[ob])
                for k2 in range(2):
                    kt = pr * 2 + k2
                    pe_.matmul(ps_d[:, :], ones_b[:, :],
                               ebuf[eb][:, k2 * 512:(k2 + 1) * 512],
                               start=(kt == 0), stop=(kt == npair * 2 - 1),
                               skip_group_check=True)
                    inst = pe_.matmul(
                        ps_o[ob],
                        vslot[s][:, (vtbase + kt) * 128:(vtbase + kt + 1) * 128],
                        ebuf[eb][:, k2 * 512:(k2 + 1) * 512],
                        start=(kt == 0), stop=(kt == npair * 2 - 1),
                        skip_group_check=True)
                e_done[eb] = p.inc(inst, "pe")
                if pr == npair - 1:
                    epi_part1(ob, h, stripe, e_done[eb])

            def epi_part1(ob, h, stripe, pe_all):
                p.w(dv_, "pe", pe_all)
                p.w(dv_, "ac", ln_ac[0])
                p.w(dv_, "dv")
                dcp = p.inc(dv_.tensor_copy(rowD, ps_d[:, :]), "dv")
                d_rd[0] = dcp
                p.w(ac_, "dv", dcp)
                ln_ac[0] = p.inc(ac_.activation(rowD, rowD, AF.Ln), "ac")
                p.w(ac_, "pe", bcv_pe[0])
                p.w(ac_, "ac")
                eneg = p.inc(ac_.activation(rowR, rowD, AF.Exp, scale=-1.0), "ac")
                hcol = h * T + (0 if stripe == "A" else 512)
                pend2[0] = (ob, hcol, eneg, gpair[0])

            def flush_pend():
                while pend:
                    emit_dav(*pend.pop(0))
                epi_part2()

            def head_stripe(g, h, stripe, ob):
                s = g % 2
                npair = NPA if stripe == "A" else NPB
                kbase = 0 if stripe == "A" else LA
                vtbase = 0 if stripe == "A" else 16
                qoff = h * T + (0 if stripe == "A" else 512)
                bias_sb = biasA_sb if stripe == "A" else biasB_sb
                p.w(pe_, f"xk{s}", slot_load[s][0])
                p.w(pe_, f"d{4 + s}", slot_load[s][1])
                for pr in range(npair):
                    sb = pr % 2
                    eb = gpair[0] % 4
                    gpair[0] += 1
                    p.w(pe_, "ac", ps_s_rd[sb])
                    for k2 in range(2):
                        kt = pr * 2 + k2
                        inst = mm(ps_s[sb][:, k2 * 512:(k2 + 1) * 512],
                                  kslot[s][:, kbase + kt * 128: kbase + (kt + 1) * 128],
                                  QSB[:, qoff: qoff + 512], True, True)
                    pev = p.inc(inst, "pe")
                    if len(pend) >= 4:
                        emit_dav(*pend.pop(0))
                    p.w(ac_, "pe", pev)
                    p.w(ac_, "pe", e_done[eb])
                    acv = p.inc(ac_.activation(
                        ebuf[eb][:, :], ps_s[sb][:, :], AF.Exp,
                        bias=bias_sb[:, pr:pr + 1], scale=INV_SQRT_D), "ac")
                    ps_s_rd[sb] = acv
                    nk, nv = "ac", acv
                    if pr < 2:
                        p.w(dv_, "ac", acv)
                        nv = p.inc(dv_.tensor_tensor(
                            ebuf[eb][:, :], ebuf[eb][:, :],
                            masks[:, pr * 1024:(pr + 1) * 1024], ALU.mult), "dv")
                        nk = "dv"
                    pend.append((s, ob, vtbase, npair, h, stripe, pr, eb, nk, nv))
                    epi_part2(force=False)

            hs = 0
            for g in range(G):
                if g + 1 < G:
                    kv_fetch(g + 1)
                for h in range(4 * g, 4 * g + 4):
                    for stripe in ("A", "B"):
                        head_stripe(g, h, stripe, hs % 2)
                        hs += 1
                flush_pend()
                slot_busy[g % 2] = p.v("pe")
            flush_pend()

        if debug:
            p.w(sy, "dv")
            p.inc(sy.dma_start(dbg["d_h1"][:, :], ARF[:, :]), "qs", 16)

        fence()

        R = rowbuf
        mean, var, arow, brow = (R[:, 512:1536], R[:, 1536:2560],
                                 R[:, 2560:3584], R[:, 3584:4608])

        def ln_stats(chunk_fn, nchunks, ps_sum, ps_sq, sqsls, xbsls):
            sq_pe = [0, 0]
            for c in range(nchunks):
                ch = chunk_fn(c)
                sb = c % 2
                p.w(ac_, "pe", sq_pe[sb])
                xcv = p.inc(ac_.copy(xbsls[sb], ch), "ac")
                p.w(dv_, "pe", sq_pe[sb])
                p.w(dv_, "dv")
                dcv = p.inc(dv_.tensor_tensor(sqsls[sb], ch, ch, ALU.mult), "dv")
                p.w(pe_, "ac", xcv)
                for half in range(2):
                    inst = mm(ps_sum[:, half * 512:(half + 1) * 512], ones_b[:, :],
                              xbsls[sb][:, half * 512:(half + 1) * 512],
                              c == 0, c == nchunks - 1)
                p.inc(inst, "pe")
                p.w(pe_, "dv", dcv)
                for half in range(2):
                    inst = mm(ps_sq[:, half * 512:(half + 1) * 512], ones_b[:, :],
                              sqsls[sb][:, half * 512:(half + 1) * 512],
                              c == 0, c == nchunks - 1)
                sq_pe[sb] = p.inc(inst, "pe")

        def ln_rows(ps_sum, ps_sq):
            p.w(ac_, "pe")
            p.inc(ac_.mul(mean, ps_sum[:, :], 1.0 / E), "ac")
            p.w(dv_, "ac")
            p.inc(dv_.tensor_tensor(var, mean, mean, ALU.mult), "dv")
            p.w(dv_, "dv")
            p.inc(dv_.scalar_tensor_tensor(var, ps_sq[:, :], 1.0 / E, var,
                                           ALU.mult, ALU.subtract), "dv")
            p.w(ac_, "dv")
            p.inc(ac_.activation(var, var, AF.Sqrt, bias=cst[:, 0:1], scale=1.0), "ac")
            p.w(dv_, "ac")
            p.inc(dv_.reciprocal(arow, var), "dv")
            p.w(dv_, "dv")
            p.inc(dv_.tensor_scalar(brow, mean, -1.0, None, ALU.mult), "dv")
            p.w(dv_, "dv")
            d5 = p.inc(dv_.tensor_tensor(brow, brow, arow, ALU.mult), "dv")
            return d5

        def ln_bcast(ps_a, ps_b, d5):
            p.w(pe_, "dv", d5)
            for half in range(2):
                pe_.matmul(ps_a[:, half * 512:(half + 1) * 512], ones_fr[:, :],
                           arow[:, half * 512:(half + 1) * 512], start=True,
                           stop=True, skip_group_check=True)
                inst = pe_.matmul(ps_b[:, half * 512:(half + 1) * 512], ones_fr[:, :],
                                  brow[:, half * 512:(half + 1) * 512], start=True,
                                  stop=True, skip_group_check=True)
            return p.inc(inst, "pe")

        subf = AR1[:, 18432:18448]
        p.w(dv_, "dv")
        p.inc(dv_.tensor_scalar(subf, suug[:, :], 1.0, None, ALU.mult), "dv")
        for c in range(KC):
            p.w(dv_, "dv")
            p.inc(dv_.tensor_scalar(masks[:, c * 128:(c + 1) * 128],
                                    h128[:, :], suug[:, c:c + 1], None,
                                    ALU.mult), "dv")
        dv_hs = p.v("dv")

        with nc.psum_tensor("ps_st", [1, 2048], F32) as ps_st:
            ps_sum, ps_sq = ps_st[:, 0:1024], ps_st[:, 1024:2048]
            ln_stats(lambda c: ARF[:, c * T:(c + 1) * T], KC, ps_sum, ps_sq,
                     [stg_b[:, 0:1024], stg_b[:, 3072:4096]],
                     [stg_b[:, 1024:2048], stg_b[:, 2048:3072]])
            d5 = ln_rows(ps_sum, ps_sq)

        if True:
            with (nc.psum_tensor("ps_bc", [128, 2048], F32) as ps_bc,
                  nc.psum_tensor("ps_f", [128, 2048], F32) as ps_f_t):
                ps_a, ps_b = ps_bc[:, 0:1024], ps_bc[:, 1024:2048]
                ps_f = [ps_f_t[:, i * 1024:(i + 1) * 1024] for i in range(2)]

                p.w(pe_, "dv", dv_hs)
                p.w(pe_, "dv", d5)
                p.w(pe_, "ac")
                for c in range(KC):
                    inst = pe_.matmul(ps_bc[0:1, c * 128:(c + 1) * 128],
                                      subf[:, c:c + 1], h128[:, :],
                                      start=True, stop=True,
                                      skip_group_check=True)
                pe_hs = p.inc(inst, "pe")
                hsut = AR1[0:1, 16384:18432]
                p.w(dv_, "pe", pe_hs)
                dv_hsut = p.inc(dv_.tensor_copy(hsut, ps_bc[0:1, 0:2048]), "dv")
                p.w(pe_, "dv", d5)
                p.w(pe_, "dv", dv_hsut)
                for half in range(2):
                    inst = pe_.matmul(ps_a[:, half * 512:(half + 1) * 512],
                                      ones_fr[:, :],
                                      arow[:, half * 512:(half + 1) * 512],
                                      start=True, stop=True,
                                      skip_group_check=True)
                pe_bc = p.inc(inst, "pe")
                browb = rowbuf[:, 0:512].bitcast(BF16)
                p.w(dv_, "dv")
                dv_bb = p.inc(dv_.tensor_copy(browb, brow), "dv")

                tsl = [stg_b[:, 1024:2048], stg_b[:, 2048:3072]]
                f1sl = [stg_b[:, 0:1024], stg_b[:, 3072:4096]]
                tsl_done = [0, 0]
                f1_done = [0, 0]
                f1_evac = [0, 0]
                t2 = AR1[:, 0:KC * T]
                f1r = scr_f1.rearrange("c (h j) t -> h c (j t)", h=8)
                rd_q = [(0, 0, 0)] * 4
                GPS = ["wq0", "wq1", "xk0", "xk1"]

                def bounce_read(gi):
                    j0, cnt = gi * 4, 4
                    dst = t2[:, j0 * T:(j0 + cnt) * T]
                    src = f1r[:, :, j0 * T:(j0 + cnt) * T]
                    p.w(gp, "d0")
                    p.w(gp, "d1")
                    p.w(sy, "d0")
                    p.w(sy, "d1")
                    p.w(ac_, "d0")
                    p.w(ac_, "d1")
                    a = p.inc(gp.dma_start(dst[0:64, :], src[0:4]), GPS[gi], 16)
                    bq_ = p.inc(sy.dma_start(dst[64:96, :], src[4:6]), f"e{gi}", 16)
                    cq = p.inc(ac_.dma_start(dst[96:128, :], src[6:8]),
                               f"e{4 + gi}", 16)
                    rd_q[gi] = (a, bq_, cq)

                p.w(dv_, "pe", pe_bc)
                p.w(ac_, "ac")
                p.w(ac_, "dv", dv_bb)
                for c in range(KC):
                    ch = ARF[:, c * T:(c + 1) * T]
                    sb = c % 2
                    p.w(dv_, "dv")
                    dvt = p.inc(dv_.tensor_tensor(ch, ch, ps_a, ALU.mult), "dv")
                    p.w(ac_, "dv", dvt)
                    p.w(ac_, "pe", tsl_done[sb])
                    acv = p.inc(ac_.copy(tsl[sb], ch), "ac")
                    p.w(pe_, "ac", acv)
                    p.w(pe_, "dv", f1_evac[sb])
                    for half in range(2):
                        mm(ps_f[sb][:, half * 512:(half + 1) * 512],
                           masks[:, c * 128:(c + 1) * 128],
                           tsl[sb][:, half * 512:(half + 1) * 512], True, False)
                        inst = pe_.matmul(
                            ps_f[sb][:, half * 512:(half + 1) * 512],
                            hsut[:, c * 128:(c + 1) * 128],
                            browb[:, half * 512:(half + 1) * 512],
                            start=False, stop=True, skip_group_check=True)
                    pev = p.inc(inst, "pe")
                    tsl_done[sb] = pev
                    p.w(dv_, "pe", pev)
                    p.w(dv_, f"d{sb}", f1_done[sb])
                    dvv = p.inc(dv_.tensor_copy(f1sl[sb], ps_f[sb][:, :]), "dv")
                    f1_evac[sb] = dvv
                    p.w(sy, "dv", dvv)
                    f1_done[sb] = p.inc(sy.dma_start(scr_f1[c][:, :], f1sl[sb]), f"d{sb}", 16)

                for gi in range(4):
                    bounce_read(gi)

                p.w(sy, "dv")
                p.inc(sy.dma_start(scr_xh[:, :], ARF[:, :]), "d7", 16)
                if debug:
                    p.inc(sy.dma_start(dbg["d_xh"][:, :], ARF[:, :]), "qs", 16)

                st2_evac = [0, 0]
                t2_ready = [0] * KC
                groups = [(0, 4), (4, 4), (8, 4), (12, 4)]
                for gi, (j0, cnt) in enumerate(groups):
                    p.w(pe_, GPS[gi], rd_q[gi][0])
                    p.w(pe_, f"e{gi}", rd_q[gi][1])
                    p.w(pe_, f"e{4 + gi}", rd_q[gi][2])
                    for j in range(j0, j0 + cnt):
                        sb = j % 2
                        p.w(pe_, "dv", st2_evac[sb])
                        for half in range(2):
                            inst = mm(ps_f[sb][:, half * 512:(half + 1) * 512],
                                      st1[:, :],
                                      t2[:, j * T + half * 512: j * T + (half + 1) * 512],
                                      True, True)
                        pev = p.inc(inst, "pe")
                        p.w(dv_, "pe", pev)
                        st2_evac[sb] = p.inc(
                            dv_.tensor_copy(t2[:, j * T:(j + 1) * T],
                                            ps_f[sb][:, :]), "dv")
                        t2_ready[j] = st2_evac[sb]

        if debug:
            p.w(sy, "dv")
            p.inc(sy.dma_start(dbg["d_t2"][:, :], t2[:, :]), "qs", 16)
            fence()

        st2_last = p.v("dv")

        wup_ring = [WBUF[:, 0:2048], QSB[:, 0:2048], QSB[:, 8192:10240]]
        wgt_ring = [WBUF[:, 2048:4096], QSB[:, 2048:4096], QSB[:, 10240:12288]]
        a_ring = [QSB[:, 4096 + s * 1024: 4096 + (s + 1) * 1024] for s in range(2)]
        b1_ring = [QSB[:, 6144 + s * 1024: 6144 + (s + 1) * 1024] for s in range(2)]
        gsl_ring = [stg_b[:, s * 1024:(s + 1) * 1024] for s in range(2)]
        wld = [0, 0, 0]
        wfree = [0, 0, 0]

        def wupgate_load(buf, c):
            p.w(gp, "pe", wfree[buf])
            sem = ["ug0", "ug1", "ug2"][buf]
            p.inc(gp.dma_start(wup_ring[buf], WupT[c][:, :]), sem, 16)
            wld[buf] = p.inc(gp.dma_start(wgt_ring[buf], WgateT[c][:, :]), sem, 16)

        with (nc.psum_tensor("ps_ug_a", [128, 2048], F32) as ps_ug_a,
              nc.psum_tensor("ps_ug_b", [128, 2048], F32) as ps_ug_b):
            PS = [ps_ug_a[:, 0:1024], ps_ug_b[:, 0:1024],
                  ps_ug_a[:, 1024:2048], ps_ug_b[:, 1024:2048]]
            pu_free = [0, 0]
            pg_free = [0, 0]
            a_done = [0, 0]
            gsl_done = [0, 0]
            b1_qs = [0, 0]
            pend = []

            def emit_stage1(c, buf, dvv, acv):
                pu = PS[2 * buf]
                p.w(pe_, "dv", dvv)
                p.w(pe_, "ac", acv)
                for half in range(2):
                    inst = mm(pu[:, half * 512:(half + 1) * 512], h128[:, :],
                              a_ring[buf][:, half * 512:(half + 1) * 512], True, True)
                pev = p.inc(inst, "pe")
                a_done[buf] = pev
                p.w(dv_, "pe", pev)
                p.w(dv_, f"d{buf}", b1_qs[buf])
                dvv = p.inc(dv_.tensor_copy(b1_ring[buf], pu[:, :]), "dv")
                pu_free[buf] = dvv
                p.w(sy, "dv", dvv)
                b1_qs[buf] = p.inc(sy.dma_start(scr_f2[c][:, :], b1_ring[buf]), f"d{buf}", 16)

            wupgate_load(0, 0)
            wupgate_load(1, 1)
            wupgate_load(2, 2)
            p.w(ac_, "d0")
            p.w(ac_, "d1")
            for c in range(IC):
                buf = c % 3
                pb = c % 2
                pu, pg = PS[2 * pb], PS[2 * pb + 1]
                p.w(pe_, ["ug0", "ug1", "ug2"][buf], wld[buf])
                p.w(pe_, "dv", pu_free[pb])
                if c == 1:
                    p.w(pe_, "dv", st2_last)
                for kc in range(KC):
                    if c == 0:
                        p.w(pe_, "dv", t2_ready[kc])
                    for half in range(2):
                        inst = mm(pu[:, half * 512:(half + 1) * 512],
                                  wup_ring[buf][:, kc * 128:(kc + 1) * 128],
                                  t2[:, kc * T + half * 512: kc * T + (half + 1) * 512],
                                  kc == 0, kc == KC - 1)
                p.inc(inst, "pe")
                if pend:
                    emit_stage1(*pend.pop(0))
                p.w(pe_, "ac", pg_free[pb])
                for kc in range(KC):
                    for half in range(2):
                        inst = mm(pg[:, half * 512:(half + 1) * 512],
                                  wgt_ring[buf][:, kc * 128:(kc + 1) * 128],
                                  t2[:, kc * T + half * 512: kc * T + (half + 1) * 512],
                                  kc == 0, kc == KC - 1)
                pe_gt = p.inc(inst, "pe")
                wfree[buf] = pe_gt
                if c + 3 < IC:
                    wupgate_load(buf, c + 3)
                p.w(ac_, "pe", pe_gt)
                p.w(ac_, "dv", gsl_done[pb])
                acv = p.inc(ac_.activation(gsl_ring[pb], pg[:, :], AF.Silu,
                                           bias=0.0, scale=sgate[:, c:c + 1]), "ac")
                pg_free[pb] = acv
                p.w(dv_, "ac", acv)
                p.w(dv_, "pe", a_done[pb])
                dvv = p.inc(dv_.scalar_tensor_tensor(
                    a_ring[pb], pu[:, :], scombo[:, c:c + 1], gsl_ring[pb],
                    ALU.mult, ALU.mult), "dv")
                gsl_done[pb] = dvv
                pend.append((c, pb, dvv, acv))
            while pend:
                emit_stage1(*pend.pop(0))

        fence()

        wd_ring = [QSB[:, s * 8192:(s + 1) * 8192] for s in range(2)]
        wd_ld = [0, 0]
        wd_free = [0, 0]

        def wd_load(buf, m):
            p.w(gp, "pe", wd_free[buf])
            wd_ld[buf] = p.inc(gp.dma_start(
                wd_ring[buf], WdownT[m][:, :]), f"wd{buf}", 16)


        def bt_chunk(j):
            if j < 32:
                return ARFb[:, j * T:(j + 1) * T]
            if j < 58:
                return AR1[:, (j - 32) * T:(j - 31) * T]
            if j < 62:
                return WBUF[:, (j - 58) * T:(j - 57) * T]
            return masks[:, (j - 62) * T:(j - 61) * T]

        f2r = scr_f2.rearrange("c (g j) t -> g c (j t)", g=2)
        with nc.psum_tensor("ps_s2", [128, 4096], F32) as ps_s2_t:
            ps_s2 = [ps_s2_t[:, i * 1024:(i + 1) * 1024] for i in range(4)]
            ev_done = [("dv", 0)] * 4
            rd_q = [(0, 0, 0)] * 4
            groups = ([(j, 4) for j in range(0, 56, 4)] +
                      [(56, 2), (58, 4), (62, 2)])

            def bt_dst(j0, cnt):
                if j0 < 32:
                    return ARFb[:, j0 * T:(j0 + cnt) * T]
                if j0 < 58:
                    return AR1[:, (j0 - 32) * T:(j0 - 32 + cnt) * T]
                if j0 < 62:
                    return WBUF[:, (j0 - 58) * T:(j0 - 58 + cnt) * T]
                return masks[:, (j0 - 62) * T:(j0 - 62 + cnt) * T]

            rd_pe4 = [0] * 4

            GPS = ["wq0", "wq1", "xk0", "xk1"]

            def f2read(gi):
                j0, cnt = groups[gi]
                sl = gi % 4
                dst = bt_dst(j0, cnt)
                src = f2r[:, :, j0 * T:(j0 + cnt) * T]
                p.w(gp, "pe", rd_pe4[sl])
                p.w(sy, "pe", rd_pe4[sl])
                p.w(ac_, "pe", rd_pe4[sl])
                a = p.inc(gp.dma_start(dst[0:64, :], src[0]), GPS[sl], 16)
                bq_ = p.inc(sy.dma_start(dst[64:96, :], src[1, 0:32]), f"d{sl}", 16)
                cq = p.inc(ac_.dma_start(dst[96:128, :], src[1, 32:64]),
                           f"d{4 + sl}", 16)
                rd_q[sl] = (a, bq_, cq)

            p.w(ac_, "ac")
            for gi in range(4):
                f2read(gi)
            p.w(gp, "pe")
            p.w(gp, "d0")
            p.w(gp, "d1")
            wd_load(0, 0)
            wd_load(1, 1)
            for gi, (j0, cnt) in enumerate(groups):
                sl = gi % 4
                p.w(pe_, GPS[sl], rd_q[sl][0])
                p.w(pe_, f"d{sl}", rd_q[sl][1])
                p.w(pe_, f"d{4 + sl}", rd_q[sl][2])
                last_pe = 0
                for j in range(j0, j0 + cnt):
                    pb = j % 4
                    p.w(pe_, *ev_done[pb])
                    for half in range(2):
                        inst = mm(ps_s2[pb][:, half * 512:(half + 1) * 512],
                                  st2[:, :],
                                  bt_chunk(j)[:, half * 512:(half + 1) * 512],
                                  True, True)
                    pev = p.inc(inst, "pe")
                    last_pe = pev
                    eng, sem = (dv_, "dv") if j % 2 == 0 else (ac_, "ac")
                    p.w(eng, "pe", pev)
                    if j % 2 == 0:
                        v = p.inc(eng.tensor_copy(bt_chunk(j), ps_s2[pb][:, :]), sem)
                    else:
                        v = p.inc(eng.copy(bt_chunk(j), ps_s2[pb][:, :]), sem)
                    ev_done[pb] = (sem, v)
                rd_pe4[sl] = last_pe
                if gi + 4 < len(groups):
                    f2read(gi + 4)

        if debug:
            p.w(sy, "dv")
            p.w(sy, "ac")
            for j in range(IC):
                p.inc(sy.dma_start(dbg["d_bt"][:, j * T:(j + 1) * T], bt_chunk(j)),
                      "qs", 16)

        fence()


        xh_ring = [stg_f[:, s * 1024:(s + 1) * 1024] for s in range(2)]
        xh_ld = [0, 0]
        xh_free = [0, 0]

        def xh_load(buf, m):
            p.w(sy, "dv", xh_free[buf])
            xh_ld[buf] = p.inc(sy.dma_start(
                xh_ring[buf], scr_xh[:, m * T:(m + 1) * T]), f"d{4 + buf}", 16)

        with (nc.psum_tensor("ps_y", [128, 2048], F32) as ps_y_t,
              nc.psum_tensor("ps_st2", [1, 2048], F32) as ps_st2):
            ps_y = [ps_y_t[:, i * 1024:(i + 1) * 1024] for i in range(2)]
            ps_sum2, ps_sq2 = ps_st2[:, 0:1024], ps_st2[:, 1024:2048]
            y_done = [0, 0]
            h2sl = [stg_f[:, 2048 + s * 1024: 2048 + (s + 1) * 1024] for s in range(2)]
            h2_state = [[0, 0, 0] for _ in range(2)]
            sq_pe = [0]
            sq_pe2 = [0]
            pend = [None]
            h2_qs_last = [0]

            def emit_stats(m, buf, dvv):
                p.w(sy, "dv", dvv)
                q = p.inc(sy.dma_start(scr_h2[m][:, :], h2sl[buf]), f"d{buf}", 16)
                h2_qs_last[0] = q
                sqsl = stg_b[:, buf * 1024:(buf + 1) * 1024]
                xbsl = stg_b[:, (2 + buf) * 1024:(3 + buf) * 1024]
                p.w(ac_, "pe", sq_pe[0] if buf == 0 else sq_pe2[0])
                p.w(ac_, "dv", dvv)
                p.inc(ac_.copy(xbsl, h2sl[buf]), "ac")
                acv = p.inc(ac_.activation(sqsl, h2sl[buf], AF.Square), "ac")
                p.w(pe_, "ac", acv)
                for half in range(2):
                    inst = mm(ps_sum2[:, half * 512:(half + 1) * 512], ones_b[:, :],
                              xbsl[:, half * 512:(half + 1) * 512],
                              m == 0, m == KC - 1)
                stat_pe = p.inc(inst, "pe")
                for half in range(2):
                    inst = mm(ps_sq2[:, half * 512:(half + 1) * 512], ones_b[:, :],
                              sqsl[:, half * 512:(half + 1) * 512],
                              m == 0, m == KC - 1)
                if buf == 0:
                    sq_pe[0] = p.inc(inst, "pe")
                else:
                    sq_pe2[0] = p.inc(inst, "pe")
                h2_state[buf] = [q, sq_pe[0], acv]

            xh_load(0, 0)
            xh_load(1, 1)
            for m in range(KC):
                buf = m % 2
                p.w(pe_, f"wd{buf}", wd_ld[buf])
                p.w(pe_, "dv", y_done[buf])
                for cc in range(IC):
                    for half in range(2):
                        inst = mm(ps_y[buf][:, half * 512:(half + 1) * 512],
                                  wd_ring[buf][:, cc * 128:(cc + 1) * 128],
                                  bt_chunk(cc)[:, half * 512:(half + 1) * 512],
                                  cc == 0, cc == IC - 1)
                pev = p.inc(inst, "pe")
                wd_free[buf] = pev
                if m + 2 < KC:
                    wd_load(buf, m + 2)
                if pend[0] is not None:
                    emit_stats(*pend[0])
                p.w(dv_, "pe", pev)
                p.w(dv_, f"d{4 + buf}", xh_ld[buf])
                st = h2_state[buf]
                p.w(dv_, f"d{buf}", st[0])
                p.w(dv_, "pe", st[1])
                p.w(dv_, "ac", st[2])
                dvv = p.inc(dv_.scalar_tensor_tensor(
                    h2sl[buf], ps_y[buf][:, :], sdpe[:, m:m + 1], xh_ring[buf],
                    ALU.mult, ALU.add), "dv")
                y_done[buf] = dvv
                xh_free[buf] = dvv
                if m + 2 < KC:
                    xh_load(buf, m + 2)
                pend[0] = (m, buf, dvv)
            emit_stats(*pend[0])

            p.w(sy, "dv")
            pre_h2 = [p.inc(sy.dma_start(stg_f[:, s * 1024:(s + 1) * 1024],
                                         scr_h2[s][:, :]), f"d{2 + s}", 16)
                      for s in range(2)]
            d5 = ln_rows(ps_sum2, ps_sq2)

        fence()

        with nc.psum_tensor("ps_bc2", [128, 2048], F32) as ps_bc2:
            ps_a2, ps_b2 = ps_bc2[:, 0:1024], ps_bc2[:, 1024:2048]
            pe_bc = ln_bcast(ps_a2, ps_b2, p.v("dv"))
            h2ring = [stg_f[:, s * 1024:(s + 1) * 1024] for s in range(2)]
            h2_rd = [0, 0]
            h2r_ld = [0, 0]

            def h2_fetch(buf, m):
                p.w(sy, "dv", h2_rd[buf])
                h2r_ld[buf] = p.inc(sy.dma_start(h2ring[buf], scr_h2[m][:, :]), f"d{2 + buf}", 16)

            ost = [stg_f[:, 2048:3072], stg_f[:, 3072:4096]]
            ost_qs = [0, 0]
            h2r_ld[0], h2r_ld[1] = pre_h2
            for m in range(KC):
                buf = m % 2
                p.w(dv_, f"d{2 + buf}", h2r_ld[buf])
                p.w(dv_, "pe", pe_bc)
                p.w(dv_, f"d{4 + buf}", ost_qs[buf])
                p.inc(dv_.tensor_tensor(ost[buf], h2ring[buf], ps_a2, ALU.mult), "dv")
                p.w(dv_, "dv")
                dvv = p.inc(dv_.tensor_tensor(ost[buf], ost[buf], ps_b2, ALU.add), "dv")
                h2_rd[buf] = dvv
                if m + 2 < KC:
                    h2_fetch(buf, m + 2)
                p.w(sy, "dv", dvv)
                ost_qs[buf] = p.inc(sy.dma_start(out[m][:, :], ost[buf]), f"d{4 + buf}", 16)

        for nm in p.h:
            if nm in ("pe", "dv", "ac"):
                continue
            p.w(sy if nm[0] in "dq" else gp, nm)

    return nc


def _prep_host(inputs):
    x = np.ascontiguousarray(np.asarray(inputs["L0_attn_x"]), dtype=np.float32)
    Wq = np.asarray(inputs["L0_attn_Wq"], np.float32)
    bq = np.asarray(inputs["L0_attn_bq"], np.float32)
    Wk = np.asarray(inputs["L0_attn_Wk"], np.float32)
    bk = np.asarray(inputs["L0_attn_bk"], np.float32)
    Wv = np.asarray(inputs["L0_attn_Wv"], np.float32)
    bv = np.asarray(inputs["L0_attn_bv"], np.float32)
    su_ug = np.asarray(inputs["L0_ffn_SU_ug"], np.float32)
    su_dp = np.asarray(inputs["L0_ffn_SU_dp"], np.float32)
    ws_ug = np.asarray(inputs["L0_ffn_Wscale_ug"], np.float32)
    ws_dp = np.asarray(inputs["L0_ffn_Wscale_dp"], np.float32)
    W_up = np.asarray(inputs["L0_ffn_W_up"], np.float32)
    W_gate = np.asarray(inputs["L0_ffn_W_gate"], np.float32)
    W_down = np.asarray(inputs["L0_ffn_W_down"], np.float32)

    xT = np.ascontiguousarray(x.transpose(0, 2, 1))
    WqT = np.ascontiguousarray(
        Wq.transpose(1, 0, 2).reshape(E, H * D)
        .reshape(16, 128, 16, 128).transpose(2, 1, 0, 3).reshape(H, 128, KC * 128))
    WkT = np.ascontiguousarray(Wk.transpose(1, 0, 2).reshape(E, G * D))
    WvT = np.ascontiguousarray(Wv.transpose(1, 0, 2).reshape(E, G * D))
    bq_col = np.ascontiguousarray(bq.T)
    bk_col = np.ascontiguousarray(bk.T)
    bv_row = np.ascontiguousarray(
        np.broadcast_to(bv.reshape(1, G * D), (128, G * D)))

    WupT = np.ascontiguousarray(
        W_up.T.reshape(16, 8, 16, INTER).transpose(2, 1, 0, 3).reshape(E, INTER)
        .reshape(16, 128, 64, 128).transpose(2, 1, 0, 3).reshape(IC, 128, KC * 128))
    WgateT = np.ascontiguousarray(
        W_gate.T.reshape(16, 8, 16, INTER).transpose(2, 1, 0, 3).reshape(E, INTER)
        .reshape(16, 128, 64, 128).transpose(2, 1, 0, 3).reshape(IC, 128, KC * 128))
    WdownT = np.ascontiguousarray(
        W_down.T.reshape(64, 2, 64, E).transpose(2, 1, 0, 3).reshape(INTER, E)
        .reshape(64, 128, 16, 128).transpose(2, 1, 0, 3).reshape(KC, 128, IC * 128))

    su_ug_c = np.ascontiguousarray(su_ug.reshape(KC, 128).T)
    su_dp_c = np.ascontiguousarray(su_dp.reshape(IC, 128).T)
    ws_up_c = np.ascontiguousarray(ws_ug[:INTER].reshape(IC, 128).T)
    ws_gt_c = np.ascontiguousarray(ws_ug[INTER:].reshape(IC, 128).T)
    ws_dp_c = np.ascontiguousarray(ws_dp.reshape(KC, 128).T)

    H128 = hadamard(128).astype(np.float32)
    ST1 = (np.kron(np.eye(8), hadamard(16)) * (2.0 ** -12)).astype(np.float32)
    ST2 = (np.kron(np.eye(2), hadamard(64)) * (2.0 ** -13)).astype(np.float32)
    masks = np.zeros((128, 2048), np.float32)
    for pr in range(2):
        for k2 in range(2):
            kt = pr * 2 + k2
            i = np.arange(128)[:, None]
            q = np.arange(512)[None, :]
            masks[:, pr * 1024 + k2 * 512: pr * 1024 + (k2 + 1) * 512] = \
                (kt * 128 + i <= q).astype(np.float32)

    shared = dict(WqT=WqT, WkT=WkT, WvT=WvT, bq_col=bq_col, bk_col=bk_col,
                  bv_row=bv_row, WupT=WupT, WgateT=WgateT, WdownT=WdownT,
                  su_ug=su_ug_c, su_dp=su_dp_c, ws_up=ws_up_c, ws_gate=ws_gt_c,
                  ws_dp=ws_dp_c, masks_in=masks, h128_in=H128, st1_in=ST1,
                  st2_in=ST2)

    in_maps = []
    for core in range(8):
        b, c = core // 4, core % 4
        oA, oB = 512 * c, 512 * (7 - c)
        klA, klB = oA + 512, oB + 512
        xkvA_ = np.zeros((E, LA), np.float32)
        xkvA_[:, 0:512] = xT[b][:, oA:oA + 512]
        xkvA_[:, 512:512 + oA] = xT[b][:, 0:oA]
        xkvB_ = np.zeros((E, LB), np.float32)
        xkvB_[:, 0:512] = xT[b][:, oB:oB + 512]
        xkvB_[:, 512:512 + oB] = xT[b][:, 0:oB]
        xq_ = np.ascontiguousarray(
            np.concatenate([xT[b][:, oA:oA + 512], xT[b][:, oB:oB + 512]], axis=1))
        biasA_ = np.full((128, NPA), -CEXP, np.float32)
        biasA_[:, klA // 256:] = NEG
        biasB_ = np.full((128, NPB), -CEXP, np.float32)
        biasB_[:, klB // 256:] = NEG
        m = dict(shared)
        m.update(xkvA=xkvA_, xkvB=xkvB_, xq=xq_, biasA=biasA_, biasB=biasB_)
        in_maps.append(m)
    return in_maps


def _assemble(results):
    outf = np.zeros((B, S, E), np.float32)
    for core in range(8):
        b, c = core // 4, core % 4
        oA, oB = 512 * c, 512 * (7 - c)
        sh = np.asarray(results[core]["out"], np.float32).reshape(E, T)
        outf[b, oA:oA + 512, :] = sh[:, 0:512].T
        outf[b, oB:oB + 512, :] = sh[:, 512:1024].T
    return outf


def kernel(**inputs):
    in_maps = _prep_host(inputs)
    nc = build_nc(debug=False)
    res = run_bass_kernel_spmd(nc, in_maps, core_ids=list(range(8)))
    return _assemble(res.results)


# version 20
# speedup vs baseline: 1.2053x; 1.0281x over previous
import math
import numpy as np
import ml_dtypes
from contextlib import ExitStack

import concourse.bass as bass
import concourse.mybir as mybir
from concourse.bass_utils import run_bass_kernel_spmd

F32 = mybir.dt.float32
BF16 = mybir.dt.bfloat16
AF = mybir.ActivationFunctionType
ALU = mybir.AluOpType

B, S, E, H, G, D = 2, 4096, 2048, 16, 4, 128
INTER = 8192
T = 1024
LA, LB = 2048, 4096
NKA, NKB = LA // 128, LB // 128
NPA, NPB = NKA // 2, NKB // 2
CEXP = 30.0
INV_SQRT_D = 1.0 / math.sqrt(D)
NEG = -1.0e30
KC = E // 128
IC = INTER // 128


def hadamard(n):
    h = np.array([[1.0]])
    while h.shape[0] < n:
        h = np.block([[h, h], [h, -h]])
    return h


class Prog:
    def __init__(self, nc, stack):
        self.nc = nc
        self.stack = stack
        self.h = {}
        self.n = {}

    def sem(self, *names):
        for name in names:
            self.h[name] = self.stack.enter_context(self.nc.semaphore(name))
            self.n[name] = 0

    def inc(self, inst, name, k=1):
        inst.then_inc(self.h[name], k)
        self.n[name] += k
        return self.n[name]

    DMA_SEMS = ("qs", "qw", "xk", "wq", "ug", "wd", "d0", "d1", "d2", "d3",
                "d4", "d5", "d6", "d7", "e0", "e1", "e2", "e3", "e4", "e5",
                "e6", "e7")

    def w(self, eng, name, val=None):
        v = self.n[name] if val is None else val
        if val is not None and any(name.startswith(x) for x in self.DMA_SEMS):
            assert v == 0 or v == self.n[name], \
                f"partial DMA-sem wait {name} {v} != total {self.n[name]}"
        if v > 0:
            eng.wait_ge(self.h[name], v)

    def v(self, name):
        return self.n[name]


def build_nc(debug=False):
    nc = bass.Bass("TRN2", target_bir_lowering=False)
    dt = nc.dram_tensor

    xkvA = dt("xkvA", [E, LA], F32, kind="ExternalInput")
    xkvB = dt("xkvB", [E, LB], F32, kind="ExternalInput")
    xq = dt("xq", [E, T], F32, kind="ExternalInput")
    WqT = dt("WqT", [H, 128, KC * 128], F32, kind="ExternalInput")
    WkT = dt("WkT", [E, G * D], F32, kind="ExternalInput")
    WvT = dt("WvT", [E, G * D], F32, kind="ExternalInput")
    bq_col = dt("bq_col", [128, H], F32, kind="ExternalInput")
    bk_col = dt("bk_col", [128, G], F32, kind="ExternalInput")
    bv_row = dt("bv_row", [128, G * D], F32, kind="ExternalInput")
    WupT = dt("WupT", [IC, 128, KC * 128], F32, kind="ExternalInput")
    WgateT = dt("WgateT", [IC, 128, KC * 128], F32, kind="ExternalInput")
    WdownT = dt("WdownT", [KC, 128, IC * 128], F32, kind="ExternalInput")
    su_ug = dt("su_ug", [128, KC], F32, kind="ExternalInput")
    su_dp = dt("su_dp", [128, IC], F32, kind="ExternalInput")
    ws_up = dt("ws_up", [128, IC], F32, kind="ExternalInput")
    ws_gate = dt("ws_gate", [128, IC], F32, kind="ExternalInput")
    ws_dp = dt("ws_dp", [128, KC], F32, kind="ExternalInput")
    biasA = dt("biasA", [128, NPA], F32, kind="ExternalInput")
    biasB = dt("biasB", [128, NPB], F32, kind="ExternalInput")
    masks_in = dt("masks_in", [128, 2048], F32, kind="ExternalInput")
    h128_in = dt("h128_in", [128, 128], F32, kind="ExternalInput")
    st1_in = dt("st1_in", [128, 128], F32, kind="ExternalInput")
    st2_in = dt("st2_in", [128, 128], F32, kind="ExternalInput")

    out = dt("out", [KC, 128, T], F32, kind="ExternalOutput")

    scr_k = dt("scr_k", [G, 128, LA + LB], BF16, kind="Internal")
    scr_vt = dt("scr_vt", [48, 128, 512], BF16, kind="Internal")
    scr_f1 = dt("scr_f1", [KC, 128, T], BF16, kind="Internal")
    scr_f2 = dt("scr_f2", [IC, 128, T], BF16, kind="Internal")
    scr_xh = dt("scr_xh", [128, KC * T], F32, kind="Internal")
    scr_h2 = dt("scr_h2", [KC, 128, T], F32, kind="Internal")

    dbg = {}
    if debug:
        dbg["d_q"] = dt("d_q", [128, H * T], BF16, kind="ExternalOutput")
        dbg["d_h1"] = dt("d_h1", [128, KC * T], F32, kind="ExternalOutput")
        dbg["d_xh"] = dt("d_xh", [128, KC * T], F32, kind="ExternalOutput")
        dbg["d_t2"] = dt("d_t2", [128, KC * T], BF16, kind="ExternalOutput")
        dbg["d_bt"] = dt("d_bt", [128, IC * T], BF16, kind="ExternalOutput")

    with ExitStack() as stack:
        e = stack.enter_context
        p = Prog(nc, stack)
        p.sem("qs", "qw", "pe", "dv", "ac",
              "xk0", "xk1", "xk2", "wq0", "wq1", "ug0", "ug1", "wd0", "wd1",
              "d0", "d1", "d2", "d3", "d4", "d5", "d6", "d7", "pl", "ug2",
              "e0", "e1", "e2", "e3", "e4", "e5", "e6", "e7")

        ARF = e(nc.sbuf_tensor("ARF", [128, KC * T], F32))
        ARFb = ARF[:, :].bitcast(BF16)
        AR1 = e(nc.sbuf_tensor("AR1", [128, 26624], BF16))
        QSB = e(nc.sbuf_tensor("QSB", [128, H * T], BF16))
        WBUF = e(nc.sbuf_tensor("WBUF", [128, 4096], BF16))
        masks = e(nc.sbuf_tensor("masks", [128, 2048], BF16))
        h128 = e(nc.sbuf_tensor("h128", [128, 128], BF16))
        st1 = e(nc.sbuf_tensor("st1", [128, 128], BF16))
        st2 = e(nc.sbuf_tensor("st2", [128, 128], BF16))
        ones_b = e(nc.sbuf_tensor("ones_b", [128, 1], BF16))
        ones_fr = e(nc.sbuf_tensor("ones_fr", [1, 128], F32))
        ones_fc = e(nc.sbuf_tensor("ones_fc", [128, 1], F32))
        bqc = e(nc.sbuf_tensor("bqc", [128, H], F32))
        bkc = e(nc.sbuf_tensor("bkc", [128, G], F32))
        bvr = e(nc.sbuf_tensor("bvr", [128, G * D], F32))
        biasA_sb = e(nc.sbuf_tensor("biasA_sb", [128, NPA], F32))
        biasB_sb = e(nc.sbuf_tensor("biasB_sb", [128, NPB], F32))
        suug = e(nc.sbuf_tensor("suug", [128, KC], F32))
        sudp = e(nc.sbuf_tensor("sudp", [128, IC], F32))
        wsup = e(nc.sbuf_tensor("wsup", [128, IC], F32))
        wsgt = e(nc.sbuf_tensor("wsgt", [128, IC], F32))
        wsdp = e(nc.sbuf_tensor("wsdp", [128, KC], F32))
        scombo = e(nc.sbuf_tensor("scombo", [128, IC], F32))
        sgate = e(nc.sbuf_tensor("sgate", [128, IC], F32))
        sdpe = e(nc.sbuf_tensor("sdpe", [128, KC], F32))
        stg_b = e(nc.sbuf_tensor("stg_b", [128, 4096], BF16))
        stg_f = e(nc.sbuf_tensor("stg_f", [128, 4096], F32))
        rowbuf = e(nc.sbuf_tensor("rowbuf", [1, 4608], F32))
        cst = e(nc.sbuf_tensor("cst", [1, 8], F32))

        wk_sb = AR1[:, 0:KC * 512]
        wv_sb = AR1[:, KC * 512:2 * KC * 512]

        sy, gp, pe_, dv_, ac_ = nc.sync, nc.gpsimd, nc.tensor, nc.vector, nc.scalar
        ENGS = [pe_, dv_, ac_, sy, gp]
        SEMS = list(p.h.keys())

        def fence():
            for eng in ENGS:
                for s in SEMS:
                    p.w(eng, s)

        def mm(out_ap, lhsT, rhs, start, stop):
            return pe_.matmul(out_ap, lhsT, rhs, start=start, stop=stop,
                              skip_group_check=True)

        for dst, src in [(bqc, bq_col), (bkc, bk_col), (bvr, bv_row),
                         (biasA_sb, biasA), (biasB_sb, biasB),
                         (suug, su_ug), (sudp, su_dp), (wsup, ws_up),
                         (wsgt, ws_gate), (wsdp, ws_dp)]:
            p.inc(sy.dma_start(dst[:, :], src[:, :]), "qs", 16)
        for dst, src in [(masks, masks_in), (h128, h128_in), (st1, st1_in),
                         (st2, st2_in)]:
            p.inc(gp.dma_start(dst[:, :], src[:, :]), "qw", 16)
        p.inc(gp.dma_start(wk_sb[:, :], WkT.rearrange("(c p) n -> p c n", p=128)), "qw", 16)
        p.inc(gp.dma_start(wv_sb[:, :], WvT.rearrange("(c p) n -> p c n", p=128)), "qw", 16)
        qw_init = p.v("qw")

        p.w(dv_, "qs")
        dv_.memset(ones_b[:, :], 1.0)
        dv_.memset(cst[:, :], 1e-5)
        dv_.memset(ones_fr[:, :], 1.0)
        p.inc(dv_.memset(ones_fc[:, :], 1.0), "dv")
        C64R2 = 64.0 * math.sqrt(2.0)
        p.inc(dv_.tensor_scalar(sgate[:, :], wsgt[:, :], C64R2, None, ALU.mult), "dv")
        p.inc(dv_.tensor_scalar(scombo[:, :], wsup[:, :], C64R2, None, ALU.mult), "dv")
        p.w(dv_, "dv")
        p.inc(dv_.tensor_tensor(scombo[:, :], scombo[:, :], sudp[:, :], ALU.mult), "dv")
        p.inc(dv_.tensor_scalar(sdpe[:, :], wsdp[:, :], C64R2, None, ALU.mult), "dv")

        XS = KC * 512
        xslot = [ARFb[:, i * XS:(i + 1) * XS] for i in range(3)]
        xslot_done = [0, 0, 0]
        chunks = [("A", 0), ("B", 0), ("A", 1), ("A", 2), ("A", 3)] + \
                 [("B", j) for j in range(1, 8)]
        qw_at = [0] * len(chunks)
        kv_col = {"A": 0, "B": LA}

        def xkv_load(ci):
            stream, idx = chunks[ci]
            slot = ci % 3
            src = (xkvA if stream == "A" else xkvB)
            sv = src.rearrange("(c p) n -> p c n", p=128)[:, :, idx * 512:(idx + 1) * 512]
            p.w(gp, "pe", xslot_done[slot])
            qw_at[ci] = p.inc(gp.dma_start(
                xslot[slot].rearrange("p (c n) -> p c n", c=KC), sv), f"xk{slot}", 16)

        with nc.psum_tensor("ps_ring", [128, 2048], F32) as ps_ring_t:
            ps_ring = [ps_ring_t[:, i * 512:(i + 1) * 512] for i in range(4)]
            ps_free = [0, 0, 0, 0]
            ring_i = [0]

            def ring_next():
                i = ring_i[0] % 4
                ring_i[0] += 1
                return i

            stg = [stg_b[:, s * 512:(s + 1) * 512] for s in range(4)]
            stg_done = [0, 0, 0, 0]
            stg_i = [0]

            def evac_spill(pev, psd, bias_kind, bias_ap, dst_dram):
                si = stg_i[0] % 4
                stg_i[0] += 1
                p.w(dv_, "pe", pev)
                p.w(dv_, f"d{si}", stg_done[si])
                if bias_kind == "scalar":
                    dvv = p.inc(dv_.tensor_scalar(stg[si], psd, bias_ap, None, ALU.add), "dv")
                else:
                    dvv = p.inc(dv_.tensor_tensor(stg[si], psd, bias_ap, ALU.add), "dv")
                p.w(sy, "dv", dvv)
                stg_done[si] = p.inc(sy.dma_start(dst_dram, stg[si]), f"d{si}", 16)
                return dvv

            def kv_chunk(ci):
                stream, idx = chunks[ci]
                slot = ci % 3
                tcol = idx * 512
                p.w(pe_, "qw", qw_init)
                p.w(pe_, f"xk{slot}", qw_at[ci])
                for g in range(G):
                    pi = ring_next()
                    psd = ps_ring[pi]
                    p.w(pe_, "dv", ps_free[pi])
                    for kc in range(KC):
                        inst = mm(psd, wk_sb[:, kc * 512 + g * 128: kc * 512 + (g + 1) * 128],
                                  xslot[slot][:, kc * 512:(kc + 1) * 512],
                                  kc == 0, kc == KC - 1)
                    pev = p.inc(inst, "pe")
                    ps_free[pi] = evac_spill(
                        pev, psd, "scalar", bkc[:, g:g + 1],
                        scr_k[g][:, kv_col[stream] + tcol: kv_col[stream] + tcol + 512])
                for tt in range(4):
                    pi = ring_next()
                    psd = ps_ring[pi]
                    p.w(pe_, "dv", ps_free[pi])
                    for kc in range(KC):
                        inst = mm(psd,
                                  xslot[slot][:, kc * 512 + tt * 128: kc * 512 + (tt + 1) * 128],
                                  wv_sb[:, kc * 512:(kc + 1) * 512],
                                  kc == 0, kc == KC - 1)
                    pev = p.inc(inst, "pe")
                    vt_idx = (idx * 4 + tt) if stream == "A" else (16 + idx * 4 + tt)
                    ps_free[pi] = evac_spill(pev, psd, "tensor", bvr[:, :],
                                             scr_vt[vt_idx][:, :])
                xslot_done[slot] = p.v("pe")

            def q_proj():
                wq_done = [0, 0]
                wq_at = [0, 0]
                wqsl = [WBUF[:, s * 2048:(s + 1) * 2048] for s in range(2)]

                def wq_load(buf, dc):
                    p.w(gp, "pe", wq_done[buf])
                    wq_at[buf] = p.inc(gp.dma_start(
                        wqsl[buf], WqT[dc][:, :]), f"wq{buf}", 16)

                wq_load(0, 0)
                wq_load(1, 1)
                for dc in range(H):
                    buf = dc % 2
                    p.w(pe_, f"wq{buf}", wq_at[buf])
                    for half, slot in ((0, 0), (1, 1)):
                        pi = ring_next()
                        psd = ps_ring[pi]
                        p.w(pe_, "dv", ps_free[pi])
                        for kc in range(KC):
                            inst = mm(psd, wqsl[buf][:, kc * 128:(kc + 1) * 128],
                                      xslot[slot][:, kc * 512:(kc + 1) * 512],
                                      kc == 0, kc == KC - 1)
                        pev = p.inc(inst, "pe")
                        p.w(dv_, "pe", pev)
                        ps_free[pi] = p.inc(dv_.tensor_scalar(
                            QSB[:, dc * T + half * 512: dc * T + (half + 1) * 512],
                            psd, bqc[:, dc:dc + 1], None, ALU.add), "dv")
                    wq_done[buf] = p.v("pe")
                    if dc + 2 < H:
                        wq_load(buf, dc + 2)

            xkv_load(0)
            xkv_load(1)
            xkv_load(2)
            for ci in range(len(chunks)):
                kv_chunk(ci)
                if ci == 1:
                    q_proj()
                    xslot_done[0] = p.v("pe")
                    xslot_done[1] = p.v("pe")
                    xkv_load(3)
                    xkv_load(4)
                elif ci >= 2 and ci + 3 < len(chunks):
                    xkv_load(ci + 3)

        p.w(sy, "pe")
        xresid_qs = p.inc(sy.dma_start(
            ARF[:, :].rearrange("p (c n) -> p c n", c=KC),
            xq.rearrange("(c p) n -> p c n", p=128)), "d6", 16)

        if debug:
            p.w(sy, "dv")
            p.inc(sy.dma_start(dbg["d_q"][:, :], QSB[:, :]), "qs", 16)

        fence()

        kslot = [AR1[:, s * 12288: s * 12288 + 6144] for s in range(2)]
        vslot = [AR1[:, s * 12288 + 6144:(s + 1) * 12288] for s in range(2)]
        ebuf = [AR1[:, 24576 + i * 1024: 24576 + (i + 1) * 1024] for i in range(2)] \
            + [stg_b[:, 0:1024], stg_b[:, 1024:2048]]
        slot_load = [(0, 0), (0, 0)]
        slot_busy = [p.v("pe"), p.v("pe")]

        def kv_fetch(g):
            s = g % 2
            vsrc = scr_vt.rearrange("t p n -> p t n")[:, :, g * 128:(g + 1) * 128]
            for eng in (sy, gp):
                p.w(eng, "pe", slot_busy[s])
                for nm in ["d0", "d1", "d2", "d3"]:
                    p.w(eng, nm)
            p.inc(gp.dma_start(kslot[s][0:64, :], scr_k[g][0:64, :]), f"xk{s}", 16)
            a2 = p.inc(gp.dma_start(vslot[s][0:64, :], vsrc[0:64]), f"xk{s}", 16)
            p.inc(sy.dma_start(kslot[s][64:128, :], scr_k[g][64:128, :]),
                  f"d{4 + s}", 16)
            b2 = p.inc(sy.dma_start(vslot[s][64:128, :], vsrc[64:128]), f"d{4 + s}", 16)
            slot_load[s] = (a2, b2)

        kv_fetch(0)

        with (nc.psum_tensor("ps_s", [128, 2048], F32) as ps_s_t,
              nc.psum_tensor("ps_o", [128, 1024], F32) as ps_o_t,
              nc.psum_tensor("ps_rd", [128, 512], F32) as ps_rd,
              nc.psum_tensor("ps_d", [1, 512], F32) as ps_d):
            ps_s = [ps_s_t[:, i * 1024:(i + 1) * 1024] for i in range(2)]
            ps_o = [ps_o_t[:, i * 512:(i + 1) * 512] for i in range(2)]
            ps_s_rd = [0, 0]
            e_done = [0, 0, 0, 0]
            o_rd = [0, 0]
            d_rd = [0]
            ln_ac = [0]
            bcv_pe = [0]
            rdfree = [0]
            rowD = rowbuf[:, 0:512]
            rowR = rowbuf[:, 512:1024]
            rdb = stg_f[:, 0:512]
            pslice = stg_f[:, 512:1024]
            pend2 = [None]

            def epi_part2(force=True):
                if pend2[0] is None:
                    return
                ob, hcol, eneg, gset = pend2[0]
                if not force and gpair[0] - gset < 6:
                    return
                pend2[0] = None
                p.w(pe_, "ac", eneg)
                p.w(pe_, "dv", rdfree[0])
                bcv = p.inc(pe_.matmul(ps_rd[:, :], ones_fr[:, :], rowR,
                                       start=True, stop=True,
                                       skip_group_check=True), "pe")
                bcv_pe[0] = bcv
                p.w(dv_, "pe", bcv)
                p.w(dv_, "dv")
                rdfree[0] = p.inc(dv_.tensor_copy(rdb, ps_rd[:, :]), "dv")
                p.w(dv_, "dv")
                dvv = p.inc(dv_.scalar_tensor_tensor(
                    pslice, ps_o[ob], 1.0, rdb, ALU.mult, ALU.mult), "dv")
                p.w(dv_, "d6", xresid_qs)
                p.w(dv_, "dv")
                dvv = p.inc(dv_.tensor_tensor(
                    ARF[:, hcol:hcol + 512], ARF[:, hcol:hcol + 512], pslice,
                    ALU.add), "dv")
                o_rd[ob] = dvv

            pend = []
            gpair = [0]

            def emit_dav(s, ob, vtbase, npair, h, stripe, pr, eb, nk, nv):
                p.w(pe_, nk, nv)
                if pr == 0:
                    p.w(pe_, "dv", d_rd[0])
                    p.w(pe_, "dv", o_rd[ob])
                for k2 in range(2):
                    kt = pr * 2 + k2
                    pe_.matmul(ps_d[:, :], ones_b[:, :],
                               ebuf[eb][:, k2 * 512:(k2 + 1) * 512],
                               start=(kt == 0), stop=(kt == npair * 2 - 1),
                               skip_group_check=True)
                for k2 in range(2):
                    kt = pr * 2 + k2
                    inst = pe_.matmul(
                        ps_o[ob],
                        vslot[s][:, (vtbase + kt) * 128:(vtbase + kt + 1) * 128],
                        ebuf[eb][:, k2 * 512:(k2 + 1) * 512],
                        start=(kt == 0), stop=(kt == npair * 2 - 1),
                        skip_group_check=True)
                e_done[eb] = p.inc(inst, "pe")
                if pr == npair - 1:
                    epi_part1(ob, h, stripe, e_done[eb])

            def epi_part1(ob, h, stripe, pe_all):
                p.w(dv_, "pe", pe_all)
                p.w(dv_, "ac", ln_ac[0])
                p.w(dv_, "dv")
                dcp = p.inc(dv_.tensor_copy(rowD, ps_d[:, :]), "dv")
                d_rd[0] = dcp
                p.w(ac_, "dv", dcp)
                ln_ac[0] = p.inc(ac_.activation(rowD, rowD, AF.Ln), "ac")
                p.w(ac_, "pe", bcv_pe[0])
                p.w(ac_, "ac")
                eneg = p.inc(ac_.activation(rowR, rowD, AF.Exp, scale=-1.0), "ac")
                hcol = h * T + (0 if stripe == "A" else 512)
                pend2[0] = (ob, hcol, eneg, gpair[0])

            def flush_pend():
                while pend:
                    emit_dav(*pend.pop(0))
                epi_part2()

            def head_stripe(g, h, stripe, ob):
                s = g % 2
                npair = NPA if stripe == "A" else NPB
                kbase = 0 if stripe == "A" else LA
                vtbase = 0 if stripe == "A" else 16
                qoff = h * T + (0 if stripe == "A" else 512)
                bias_sb = biasA_sb if stripe == "A" else biasB_sb
                p.w(pe_, f"xk{s}", slot_load[s][0])
                p.w(pe_, f"d{4 + s}", slot_load[s][1])
                for pr in range(npair):
                    sb = pr % 2
                    eb = gpair[0] % 4
                    gpair[0] += 1
                    p.w(pe_, "ac", ps_s_rd[sb])
                    for k2 in range(2):
                        kt = pr * 2 + k2
                        inst = mm(ps_s[sb][:, k2 * 512:(k2 + 1) * 512],
                                  kslot[s][:, kbase + kt * 128: kbase + (kt + 1) * 128],
                                  QSB[:, qoff: qoff + 512], True, True)
                    pev = p.inc(inst, "pe")
                    if len(pend) >= 4:
                        emit_dav(*pend.pop(0))
                    p.w(ac_, "pe", pev)
                    p.w(ac_, "pe", e_done[eb])
                    acv = p.inc(ac_.activation(
                        ebuf[eb][:, :], ps_s[sb][:, :], AF.Exp,
                        bias=bias_sb[:, pr:pr + 1], scale=INV_SQRT_D), "ac")
                    ps_s_rd[sb] = acv
                    nk, nv = "ac", acv
                    if pr < 2:
                        p.w(dv_, "ac", acv)
                        nv = p.inc(dv_.tensor_tensor(
                            ebuf[eb][:, :], ebuf[eb][:, :],
                            masks[:, pr * 1024:(pr + 1) * 1024], ALU.mult), "dv")
                        nk = "dv"
                    pend.append((s, ob, vtbase, npair, h, stripe, pr, eb, nk, nv))
                    epi_part2(force=False)

            hs = 0
            for g in range(G):
                if g + 1 < G:
                    kv_fetch(g + 1)
                for h in range(4 * g, 4 * g + 4):
                    for stripe in ("A", "B"):
                        head_stripe(g, h, stripe, hs % 2)
                        hs += 1
                flush_pend()
                slot_busy[g % 2] = p.v("pe")
            flush_pend()

        if debug:
            p.w(sy, "dv")
            p.inc(sy.dma_start(dbg["d_h1"][:, :], ARF[:, :]), "qs", 16)

        fence()

        R = rowbuf
        mean, var, arow, brow = (R[:, 512:1536], R[:, 1536:2560],
                                 R[:, 2560:3584], R[:, 3584:4608])

        def ln_stats(chunk_fn, nchunks, ps_sum, ps_sq, sqsls, xbsls):
            sq_pe = [0, 0]
            for c in range(nchunks):
                ch = chunk_fn(c)
                sb = c % 2
                p.w(ac_, "pe", sq_pe[sb])
                xcv = p.inc(ac_.copy(xbsls[sb], ch), "ac")
                p.w(dv_, "pe", sq_pe[sb])
                p.w(dv_, "dv")
                dcv = p.inc(dv_.tensor_tensor(sqsls[sb], ch, ch, ALU.mult), "dv")
                p.w(pe_, "ac", xcv)
                for half in range(2):
                    inst = mm(ps_sum[:, half * 512:(half + 1) * 512], ones_b[:, :],
                              xbsls[sb][:, half * 512:(half + 1) * 512],
                              c == 0, c == nchunks - 1)
                p.inc(inst, "pe")
                p.w(pe_, "dv", dcv)
                for half in range(2):
                    inst = mm(ps_sq[:, half * 512:(half + 1) * 512], ones_b[:, :],
                              sqsls[sb][:, half * 512:(half + 1) * 512],
                              c == 0, c == nchunks - 1)
                sq_pe[sb] = p.inc(inst, "pe")

        def ln_rows(ps_sum, ps_sq):
            p.w(ac_, "pe")
            p.inc(ac_.mul(mean, ps_sum[:, :], 1.0 / E), "ac")
            p.w(dv_, "ac")
            p.inc(dv_.tensor_tensor(var, mean, mean, ALU.mult), "dv")
            p.w(dv_, "dv")
            p.inc(dv_.scalar_tensor_tensor(var, ps_sq[:, :], 1.0 / E, var,
                                           ALU.mult, ALU.subtract), "dv")
            p.w(ac_, "dv")
            p.inc(ac_.activation(var, var, AF.Sqrt, bias=cst[:, 0:1], scale=1.0), "ac")
            p.w(dv_, "ac")
            p.inc(dv_.reciprocal(arow, var), "dv")
            p.w(dv_, "dv")
            p.inc(dv_.tensor_scalar(brow, mean, -1.0, None, ALU.mult), "dv")
            p.w(dv_, "dv")
            d5 = p.inc(dv_.tensor_tensor(brow, brow, arow, ALU.mult), "dv")
            return d5

        def ln_bcast(ps_a, ps_b, d5):
            p.w(pe_, "dv", d5)
            for half in range(2):
                pe_.matmul(ps_a[:, half * 512:(half + 1) * 512], ones_fr[:, :],
                           arow[:, half * 512:(half + 1) * 512], start=True,
                           stop=True, skip_group_check=True)
                inst = pe_.matmul(ps_b[:, half * 512:(half + 1) * 512], ones_fr[:, :],
                                  brow[:, half * 512:(half + 1) * 512], start=True,
                                  stop=True, skip_group_check=True)
            return p.inc(inst, "pe")

        subf = AR1[:, 18432:18448]
        p.w(dv_, "dv")
        p.inc(dv_.tensor_scalar(subf, suug[:, :], 1.0, None, ALU.mult), "dv")
        for c in range(KC):
            p.w(dv_, "dv")
            p.inc(dv_.tensor_scalar(masks[:, c * 128:(c + 1) * 128],
                                    h128[:, :], suug[:, c:c + 1], None,
                                    ALU.mult), "dv")
        dv_hs = p.v("dv")

        with nc.psum_tensor("ps_st", [1, 2048], F32) as ps_st:
            ps_sum, ps_sq = ps_st[:, 0:1024], ps_st[:, 1024:2048]
            ln_stats(lambda c: ARF[:, c * T:(c + 1) * T], KC, ps_sum, ps_sq,
                     [stg_b[:, 0:1024], stg_b[:, 3072:4096]],
                     [stg_b[:, 1024:2048], stg_b[:, 2048:3072]])
            d5 = ln_rows(ps_sum, ps_sq)

        if True:
            with (nc.psum_tensor("ps_bc", [128, 2048], F32) as ps_bc,
                  nc.psum_tensor("ps_f", [128, 2048], F32) as ps_f_t):
                ps_a, ps_b = ps_bc[:, 0:1024], ps_bc[:, 1024:2048]
                ps_f = [ps_f_t[:, i * 1024:(i + 1) * 1024] for i in range(2)]

                p.w(pe_, "dv", dv_hs)
                p.w(pe_, "dv", d5)
                p.w(pe_, "ac")
                for c in range(KC):
                    inst = pe_.matmul(ps_bc[0:1, c * 128:(c + 1) * 128],
                                      subf[:, c:c + 1], h128[:, :],
                                      start=True, stop=True,
                                      skip_group_check=True)
                pe_hs = p.inc(inst, "pe")
                hsut = AR1[0:1, 16384:18432]
                p.w(dv_, "pe", pe_hs)
                dv_hsut = p.inc(dv_.tensor_copy(hsut, ps_bc[0:1, 0:2048]), "dv")
                p.w(pe_, "dv", d5)
                p.w(pe_, "dv", dv_hsut)
                for half in range(2):
                    inst = pe_.matmul(ps_a[:, half * 512:(half + 1) * 512],
                                      ones_fr[:, :],
                                      arow[:, half * 512:(half + 1) * 512],
                                      start=True, stop=True,
                                      skip_group_check=True)
                pe_bc = p.inc(inst, "pe")
                browb = rowbuf[:, 0:512].bitcast(BF16)
                p.w(dv_, "dv")
                dv_bb = p.inc(dv_.tensor_copy(browb, brow), "dv")

                tsl = [stg_b[:, 1024:2048], stg_b[:, 2048:3072]]
                f1sl = [stg_b[:, 0:1024], stg_b[:, 3072:4096]]
                tsl_done = [0, 0]
                f1_done = [0, 0]
                f1_evac = [0, 0]
                t2 = AR1[:, 0:KC * T]
                f1r = scr_f1.rearrange("c (h j) t -> h c (j t)", h=8)
                rd_q = [(0, 0, 0)] * 4
                GPS = ["wq0", "wq1", "xk0", "xk1"]

                def bounce_read(gi):
                    j0, cnt = gi * 4, 4
                    dst = t2[:, j0 * T:(j0 + cnt) * T]
                    src = f1r[:, :, j0 * T:(j0 + cnt) * T]
                    p.w(gp, "d0")
                    p.w(gp, "d1")
                    p.w(sy, "d0")
                    p.w(sy, "d1")
                    p.w(ac_, "d0")
                    p.w(ac_, "d1")
                    a = p.inc(gp.dma_start(dst[0:64, :], src[0:4]), GPS[gi], 16)
                    bq_ = p.inc(sy.dma_start(dst[64:96, :], src[4:6]), f"e{gi}", 16)
                    cq = p.inc(ac_.dma_start(dst[96:128, :], src[6:8]),
                               f"e{4 + gi}", 16)
                    rd_q[gi] = (a, bq_, cq)

                p.w(dv_, "pe", pe_bc)
                p.w(ac_, "ac")
                p.w(ac_, "dv", dv_bb)
                for c in range(KC):
                    ch = ARF[:, c * T:(c + 1) * T]
                    sb = c % 2
                    p.w(dv_, "dv")
                    dvt = p.inc(dv_.tensor_tensor(ch, ch, ps_a, ALU.mult), "dv")
                    p.w(ac_, "dv", dvt)
                    p.w(ac_, "pe", tsl_done[sb])
                    acv = p.inc(ac_.copy(tsl[sb], ch), "ac")
                    p.w(pe_, "ac", acv)
                    p.w(pe_, "dv", f1_evac[sb])
                    for half in range(2):
                        mm(ps_f[sb][:, half * 512:(half + 1) * 512],
                           masks[:, c * 128:(c + 1) * 128],
                           tsl[sb][:, half * 512:(half + 1) * 512], True, False)
                        inst = pe_.matmul(
                            ps_f[sb][:, half * 512:(half + 1) * 512],
                            hsut[:, c * 128:(c + 1) * 128],
                            browb[:, half * 512:(half + 1) * 512],
                            start=False, stop=True, skip_group_check=True)
                    pev = p.inc(inst, "pe")
                    tsl_done[sb] = pev
                    p.w(dv_, "pe", pev)
                    p.w(dv_, f"d{sb}", f1_done[sb])
                    dvv = p.inc(dv_.tensor_copy(f1sl[sb], ps_f[sb][:, :]), "dv")
                    f1_evac[sb] = dvv
                    p.w(sy, "dv", dvv)
                    f1_done[sb] = p.inc(sy.dma_start(scr_f1[c][:, :], f1sl[sb]), f"d{sb}", 16)

                for gi in range(4):
                    bounce_read(gi)

                p.w(sy, "dv")
                p.inc(sy.dma_start(scr_xh[:, :], ARF[:, :]), "d7", 16)
                if debug:
                    p.inc(sy.dma_start(dbg["d_xh"][:, :], ARF[:, :]), "qs", 16)

                st2_evac = [0, 0]
                t2_ready = [0] * KC
                groups = [(0, 4), (4, 4), (8, 4), (12, 4)]
                for gi, (j0, cnt) in enumerate(groups):
                    p.w(pe_, GPS[gi], rd_q[gi][0])
                    p.w(pe_, f"e{gi}", rd_q[gi][1])
                    p.w(pe_, f"e{4 + gi}", rd_q[gi][2])
                    for j in range(j0, j0 + cnt):
                        sb = j % 2
                        p.w(pe_, "dv", st2_evac[sb])
                        for half in range(2):
                            inst = mm(ps_f[sb][:, half * 512:(half + 1) * 512],
                                      st1[:, :],
                                      t2[:, j * T + half * 512: j * T + (half + 1) * 512],
                                      True, True)
                        pev = p.inc(inst, "pe")
                        p.w(dv_, "pe", pev)
                        st2_evac[sb] = p.inc(
                            dv_.tensor_copy(t2[:, j * T:(j + 1) * T],
                                            ps_f[sb][:, :]), "dv")
                        t2_ready[j] = st2_evac[sb]

        if debug:
            p.w(sy, "dv")
            p.inc(sy.dma_start(dbg["d_t2"][:, :], t2[:, :]), "qs", 16)
            fence()

        st2_last = p.v("dv")

        wup_ring = [WBUF[:, 0:2048], QSB[:, 0:2048], QSB[:, 8192:10240]]
        wgt_ring = [WBUF[:, 2048:4096], QSB[:, 2048:4096], QSB[:, 10240:12288]]
        a_ring = [QSB[:, 4096 + s * 1024: 4096 + (s + 1) * 1024] for s in range(2)]
        b1_ring = [QSB[:, 6144 + s * 1024: 6144 + (s + 1) * 1024] for s in range(2)]
        gsl_ring = [stg_b[:, s * 1024:(s + 1) * 1024] for s in range(2)]
        wld = [0, 0, 0]
        wfree = [0, 0, 0]

        def wupgate_load(buf, c):
            p.w(gp, "pe", wfree[buf])
            sem = ["ug0", "ug1", "ug2"][buf]
            p.inc(gp.dma_start(wup_ring[buf], WupT[c][:, :]), sem, 16)
            wld[buf] = p.inc(gp.dma_start(wgt_ring[buf], WgateT[c][:, :]), sem, 16)

        with (nc.psum_tensor("ps_ug_a", [128, 2048], F32) as ps_ug_a,
              nc.psum_tensor("ps_ug_b", [128, 2048], F32) as ps_ug_b):
            PS = [ps_ug_a[:, 0:1024], ps_ug_b[:, 0:1024],
                  ps_ug_a[:, 1024:2048], ps_ug_b[:, 1024:2048]]
            pu_free = [0, 0]
            pg_free = [0, 0]
            a_done = [0, 0]
            gsl_done = [0, 0]
            b1_qs = [0, 0]
            pend = []

            def emit_stage1(c, buf, dvv, acv):
                pu = PS[2 * buf]
                p.w(pe_, "dv", dvv)
                p.w(pe_, "ac", acv)
                for half in range(2):
                    inst = mm(pu[:, half * 512:(half + 1) * 512], h128[:, :],
                              a_ring[buf][:, half * 512:(half + 1) * 512], True, True)
                pev = p.inc(inst, "pe")
                a_done[buf] = pev
                p.w(dv_, "pe", pev)
                p.w(dv_, f"d{buf}", b1_qs[buf])
                dvv = p.inc(dv_.tensor_copy(b1_ring[buf], pu[:, :]), "dv")
                pu_free[buf] = dvv
                p.w(sy, "dv", dvv)
                b1_qs[buf] = p.inc(sy.dma_start(scr_f2[c][:, :], b1_ring[buf]), f"d{buf}", 16)

            wupgate_load(0, 0)
            wupgate_load(1, 1)
            wupgate_load(2, 2)
            p.w(ac_, "d0")
            p.w(ac_, "d1")
            for c in range(IC):
                buf = c % 3
                pb = c % 2
                pu, pg = PS[2 * pb], PS[2 * pb + 1]
                p.w(pe_, ["ug0", "ug1", "ug2"][buf], wld[buf])
                p.w(pe_, "dv", pu_free[pb])
                if c == 1:
                    p.w(pe_, "dv", st2_last)
                for kc in range(KC):
                    if c == 0:
                        p.w(pe_, "dv", t2_ready[kc])
                    for half in range(2):
                        inst = mm(pu[:, half * 512:(half + 1) * 512],
                                  wup_ring[buf][:, kc * 128:(kc + 1) * 128],
                                  t2[:, kc * T + half * 512: kc * T + (half + 1) * 512],
                                  kc == 0, kc == KC - 1)
                p.inc(inst, "pe")
                if pend:
                    emit_stage1(*pend.pop(0))
                p.w(pe_, "ac", pg_free[pb])
                for kc in range(KC):
                    for half in range(2):
                        inst = mm(pg[:, half * 512:(half + 1) * 512],
                                  wgt_ring[buf][:, kc * 128:(kc + 1) * 128],
                                  t2[:, kc * T + half * 512: kc * T + (half + 1) * 512],
                                  kc == 0, kc == KC - 1)
                pe_gt = p.inc(inst, "pe")
                wfree[buf] = pe_gt
                if c + 3 < IC:
                    wupgate_load(buf, c + 3)
                p.w(ac_, "pe", pe_gt)
                p.w(ac_, "dv", gsl_done[pb])
                acv = p.inc(ac_.activation(gsl_ring[pb], pg[:, :], AF.Silu,
                                           bias=0.0, scale=sgate[:, c:c + 1]), "ac")
                pg_free[pb] = acv
                p.w(dv_, "ac", acv)
                p.w(dv_, "pe", a_done[pb])
                dvv = p.inc(dv_.scalar_tensor_tensor(
                    a_ring[pb], pu[:, :], scombo[:, c:c + 1], gsl_ring[pb],
                    ALU.mult, ALU.mult), "dv")
                gsl_done[pb] = dvv
                pend.append((c, pb, dvv, acv))
            while pend:
                emit_stage1(*pend.pop(0))

        fence()

        wd_ring = [QSB[:, s * 8192:(s + 1) * 8192] for s in range(2)]
        wd_ld = [0, 0]
        wd_free = [0, 0]

        def wd_load(buf, m):
            p.w(gp, "pe", wd_free[buf])
            wd_ld[buf] = p.inc(gp.dma_start(
                wd_ring[buf], WdownT[m][:, :]), f"wd{buf}", 16)


        def bt_chunk(j):
            if j < 32:
                return ARFb[:, j * T:(j + 1) * T]
            if j < 58:
                return AR1[:, (j - 32) * T:(j - 31) * T]
            if j < 62:
                return WBUF[:, (j - 58) * T:(j - 57) * T]
            return masks[:, (j - 62) * T:(j - 61) * T]

        f2r = scr_f2.rearrange("c (g j) t -> g c (j t)", g=2)
        with nc.psum_tensor("ps_s2", [128, 4096], F32) as ps_s2_t:
            ps_s2 = [ps_s2_t[:, i * 1024:(i + 1) * 1024] for i in range(4)]
            ev_done = [("dv", 0)] * 4
            rd_q = [(0, 0, 0)] * 4
            groups = ([(j, 4) for j in range(0, 56, 4)] +
                      [(56, 2), (58, 4), (62, 2)])

            def bt_dst(j0, cnt):
                if j0 < 32:
                    return ARFb[:, j0 * T:(j0 + cnt) * T]
                if j0 < 58:
                    return AR1[:, (j0 - 32) * T:(j0 - 32 + cnt) * T]
                if j0 < 62:
                    return WBUF[:, (j0 - 58) * T:(j0 - 58 + cnt) * T]
                return masks[:, (j0 - 62) * T:(j0 - 62 + cnt) * T]

            rd_pe4 = [0] * 4

            GPS = ["wq0", "wq1", "xk0", "xk1"]

            def f2read(gi):
                j0, cnt = groups[gi]
                sl = gi % 4
                dst = bt_dst(j0, cnt)
                src = f2r[:, :, j0 * T:(j0 + cnt) * T]
                p.w(gp, "pe", rd_pe4[sl])
                p.w(sy, "pe", rd_pe4[sl])
                p.w(ac_, "pe", rd_pe4[sl])
                a = p.inc(gp.dma_start(dst[0:64, :], src[0]), GPS[sl], 16)
                bq_ = p.inc(sy.dma_start(dst[64:96, :], src[1, 0:32]), f"d{sl}", 16)
                cq = p.inc(ac_.dma_start(dst[96:128, :], src[1, 32:64]),
                           f"d{4 + sl}", 16)
                rd_q[sl] = (a, bq_, cq)

            p.w(ac_, "ac")
            for gi in range(4):
                f2read(gi)
            p.w(gp, "pe")
            p.w(gp, "d0")
            p.w(gp, "d1")
            wd_load(0, 0)
            wd_load(1, 1)
            for gi, (j0, cnt) in enumerate(groups):
                sl = gi % 4
                p.w(pe_, GPS[sl], rd_q[sl][0])
                p.w(pe_, f"d{sl}", rd_q[sl][1])
                p.w(pe_, f"d{4 + sl}", rd_q[sl][2])
                last_pe = 0
                for j in range(j0, j0 + cnt):
                    pb = j % 4
                    p.w(pe_, *ev_done[pb])
                    for half in range(2):
                        inst = mm(ps_s2[pb][:, half * 512:(half + 1) * 512],
                                  st2[:, :],
                                  bt_chunk(j)[:, half * 512:(half + 1) * 512],
                                  True, True)
                    pev = p.inc(inst, "pe")
                    last_pe = pev
                    eng, sem = (dv_, "dv") if j % 2 == 0 else (ac_, "ac")
                    p.w(eng, "pe", pev)
                    if j % 2 == 0:
                        v = p.inc(eng.tensor_copy(bt_chunk(j), ps_s2[pb][:, :]), sem)
                    else:
                        v = p.inc(eng.copy(bt_chunk(j), ps_s2[pb][:, :]), sem)
                    ev_done[pb] = (sem, v)
                rd_pe4[sl] = last_pe
                if gi + 4 < len(groups):
                    f2read(gi + 4)

        if debug:
            p.w(sy, "dv")
            p.w(sy, "ac")
            for j in range(IC):
                p.inc(sy.dma_start(dbg["d_bt"][:, j * T:(j + 1) * T], bt_chunk(j)),
                      "qs", 16)

        fence()


        xh_ring = [stg_f[:, s * 1024:(s + 1) * 1024] for s in range(2)]
        xh_ld = [0, 0]
        xh_free = [0, 0]

        def xh_load(buf, m):
            p.w(sy, "dv", xh_free[buf])
            xh_ld[buf] = p.inc(sy.dma_start(
                xh_ring[buf], scr_xh[:, m * T:(m + 1) * T]), f"d{4 + buf}", 16)

        with (nc.psum_tensor("ps_y", [128, 2048], F32) as ps_y_t,
              nc.psum_tensor("ps_st2", [1, 2048], F32) as ps_st2):
            ps_y = [ps_y_t[:, i * 1024:(i + 1) * 1024] for i in range(2)]
            ps_sum2, ps_sq2 = ps_st2[:, 0:1024], ps_st2[:, 1024:2048]
            y_done = [0, 0]
            h2sl = [stg_f[:, 2048 + s * 1024: 2048 + (s + 1) * 1024] for s in range(2)]
            h2_state = [[0, 0, 0] for _ in range(2)]
            sq_pe = [0]
            sq_pe2 = [0]
            pend = [None]
            h2_qs_last = [0]

            def emit_stats(m, buf, dvv):
                p.w(sy, "dv", dvv)
                q = p.inc(sy.dma_start(scr_h2[m][:, :], h2sl[buf]), f"d{buf}", 16)
                h2_qs_last[0] = q
                sqsl = stg_b[:, buf * 1024:(buf + 1) * 1024]
                xbsl = stg_b[:, (2 + buf) * 1024:(3 + buf) * 1024]
                p.w(ac_, "pe", sq_pe[0] if buf == 0 else sq_pe2[0])
                p.w(ac_, "dv", dvv)
                p.inc(ac_.copy(xbsl, h2sl[buf]), "ac")
                acv = p.inc(ac_.activation(sqsl, h2sl[buf], AF.Square), "ac")
                p.w(pe_, "ac", acv)
                for half in range(2):
                    inst = mm(ps_sum2[:, half * 512:(half + 1) * 512], ones_b[:, :],
                              xbsl[:, half * 512:(half + 1) * 512],
                              m == 0, m == KC - 1)
                stat_pe = p.inc(inst, "pe")
                for half in range(2):
                    inst = mm(ps_sq2[:, half * 512:(half + 1) * 512], ones_b[:, :],
                              sqsl[:, half * 512:(half + 1) * 512],
                              m == 0, m == KC - 1)
                if buf == 0:
                    sq_pe[0] = p.inc(inst, "pe")
                else:
                    sq_pe2[0] = p.inc(inst, "pe")
                h2_state[buf] = [q, sq_pe[0], acv]

            xh_load(0, 0)
            xh_load(1, 1)
            for m in range(KC):
                buf = m % 2
                p.w(pe_, f"wd{buf}", wd_ld[buf])
                p.w(pe_, "dv", y_done[buf])
                for cc in range(IC):
                    for half in range(2):
                        inst = mm(ps_y[buf][:, half * 512:(half + 1) * 512],
                                  wd_ring[buf][:, cc * 128:(cc + 1) * 128],
                                  bt_chunk(cc)[:, half * 512:(half + 1) * 512],
                                  cc == 0, cc == IC - 1)
                pev = p.inc(inst, "pe")
                wd_free[buf] = pev
                if m + 2 < KC:
                    wd_load(buf, m + 2)
                if pend[0] is not None:
                    emit_stats(*pend[0])
                p.w(dv_, "pe", pev)
                p.w(dv_, f"d{4 + buf}", xh_ld[buf])
                st = h2_state[buf]
                p.w(dv_, f"d{buf}", st[0])
                p.w(dv_, "pe", st[1])
                p.w(dv_, "ac", st[2])
                dvv = p.inc(dv_.scalar_tensor_tensor(
                    h2sl[buf], ps_y[buf][:, :], sdpe[:, m:m + 1], xh_ring[buf],
                    ALU.mult, ALU.add), "dv")
                y_done[buf] = dvv
                xh_free[buf] = dvv
                if m + 2 < KC:
                    xh_load(buf, m + 2)
                pend[0] = (m, buf, dvv)
            emit_stats(*pend[0])

            p.w(sy, "dv")
            pre_h2 = [p.inc(sy.dma_start(stg_f[:, s * 1024:(s + 1) * 1024],
                                         scr_h2[s][:, :]), f"d{2 + s}", 16)
                      for s in range(2)]
            d5 = ln_rows(ps_sum2, ps_sq2)

        fence()

        with nc.psum_tensor("ps_bc2", [128, 2048], F32) as ps_bc2:
            ps_a2, ps_b2 = ps_bc2[:, 0:1024], ps_bc2[:, 1024:2048]
            pe_bc = ln_bcast(ps_a2, ps_b2, p.v("dv"))
            h2ring = [stg_f[:, s * 1024:(s + 1) * 1024] for s in range(2)]
            h2_rd = [0, 0]
            h2r_ld = [0, 0]

            def h2_fetch(buf, m):
                p.w(sy, "dv", h2_rd[buf])
                h2r_ld[buf] = p.inc(sy.dma_start(h2ring[buf], scr_h2[m][:, :]), f"d{2 + buf}", 16)

            ost = [stg_f[:, 2048:3072], stg_f[:, 3072:4096]]
            ost_qs = [0, 0]
            h2r_ld[0], h2r_ld[1] = pre_h2
            for m in range(KC):
                buf = m % 2
                p.w(dv_, f"d{2 + buf}", h2r_ld[buf])
                p.w(dv_, "pe", pe_bc)
                p.w(dv_, f"d{4 + buf}", ost_qs[buf])
                p.inc(dv_.tensor_tensor(ost[buf], h2ring[buf], ps_a2, ALU.mult), "dv")
                p.w(dv_, "dv")
                dvv = p.inc(dv_.tensor_tensor(ost[buf], ost[buf], ps_b2, ALU.add), "dv")
                h2_rd[buf] = dvv
                if m + 2 < KC:
                    h2_fetch(buf, m + 2)
                p.w(sy, "dv", dvv)
                ost_qs[buf] = p.inc(sy.dma_start(out[m][:, :], ost[buf]), f"d{4 + buf}", 16)

        for nm in p.h:
            if nm in ("pe", "dv", "ac"):
                continue
            p.w(sy if nm[0] in "dq" else gp, nm)

    return nc


def _prep_host(inputs):
    x = np.ascontiguousarray(np.asarray(inputs["L0_attn_x"]), dtype=np.float32)
    Wq = np.asarray(inputs["L0_attn_Wq"], np.float32)
    bq = np.asarray(inputs["L0_attn_bq"], np.float32)
    Wk = np.asarray(inputs["L0_attn_Wk"], np.float32)
    bk = np.asarray(inputs["L0_attn_bk"], np.float32)
    Wv = np.asarray(inputs["L0_attn_Wv"], np.float32)
    bv = np.asarray(inputs["L0_attn_bv"], np.float32)
    su_ug = np.asarray(inputs["L0_ffn_SU_ug"], np.float32)
    su_dp = np.asarray(inputs["L0_ffn_SU_dp"], np.float32)
    ws_ug = np.asarray(inputs["L0_ffn_Wscale_ug"], np.float32)
    ws_dp = np.asarray(inputs["L0_ffn_Wscale_dp"], np.float32)
    W_up = np.asarray(inputs["L0_ffn_W_up"], np.float32)
    W_gate = np.asarray(inputs["L0_ffn_W_gate"], np.float32)
    W_down = np.asarray(inputs["L0_ffn_W_down"], np.float32)

    xT = np.ascontiguousarray(x.transpose(0, 2, 1))
    WqT = np.ascontiguousarray(
        Wq.transpose(1, 0, 2).reshape(E, H * D)
        .reshape(16, 128, 16, 128).transpose(2, 1, 0, 3).reshape(H, 128, KC * 128))
    WkT = np.ascontiguousarray(Wk.transpose(1, 0, 2).reshape(E, G * D))
    WvT = np.ascontiguousarray(Wv.transpose(1, 0, 2).reshape(E, G * D))
    bq_col = np.ascontiguousarray(bq.T)
    bk_col = np.ascontiguousarray(bk.T)
    bv_row = np.ascontiguousarray(
        np.broadcast_to(bv.reshape(1, G * D), (128, G * D)))

    WupT = np.ascontiguousarray(
        W_up.T.reshape(16, 8, 16, INTER).transpose(2, 1, 0, 3).reshape(E, INTER)
        .reshape(16, 128, 64, 128).transpose(2, 1, 0, 3).reshape(IC, 128, KC * 128))
    WgateT = np.ascontiguousarray(
        W_gate.T.reshape(16, 8, 16, INTER).transpose(2, 1, 0, 3).reshape(E, INTER)
        .reshape(16, 128, 64, 128).transpose(2, 1, 0, 3).reshape(IC, 128, KC * 128))
    WdownT = np.ascontiguousarray(
        W_down.T.reshape(64, 2, 64, E).transpose(2, 1, 0, 3).reshape(INTER, E)
        .reshape(64, 128, 16, 128).transpose(2, 1, 0, 3).reshape(KC, 128, IC * 128))

    su_ug_c = np.ascontiguousarray(su_ug.reshape(KC, 128).T)
    su_dp_c = np.ascontiguousarray(su_dp.reshape(IC, 128).T)
    ws_up_c = np.ascontiguousarray(ws_ug[:INTER].reshape(IC, 128).T)
    ws_gt_c = np.ascontiguousarray(ws_ug[INTER:].reshape(IC, 128).T)
    ws_dp_c = np.ascontiguousarray(ws_dp.reshape(KC, 128).T)

    H128 = hadamard(128).astype(np.float32)
    ST1 = (np.kron(np.eye(8), hadamard(16)) * (2.0 ** -12)).astype(np.float32)
    ST2 = (np.kron(np.eye(2), hadamard(64)) * (2.0 ** -13)).astype(np.float32)
    masks = np.zeros((128, 2048), np.float32)
    for pr in range(2):
        for k2 in range(2):
            kt = pr * 2 + k2
            i = np.arange(128)[:, None]
            q = np.arange(512)[None, :]
            masks[:, pr * 1024 + k2 * 512: pr * 1024 + (k2 + 1) * 512] = \
                (kt * 128 + i <= q).astype(np.float32)

    shared = dict(WqT=WqT, WkT=WkT, WvT=WvT, bq_col=bq_col, bk_col=bk_col,
                  bv_row=bv_row, WupT=WupT, WgateT=WgateT, WdownT=WdownT,
                  su_ug=su_ug_c, su_dp=su_dp_c, ws_up=ws_up_c, ws_gate=ws_gt_c,
                  ws_dp=ws_dp_c, masks_in=masks, h128_in=H128, st1_in=ST1,
                  st2_in=ST2)

    in_maps = []
    for core in range(8):
        b, c = core // 4, core % 4
        oA, oB = 512 * c, 512 * (7 - c)
        klA, klB = oA + 512, oB + 512
        xkvA_ = np.zeros((E, LA), np.float32)
        xkvA_[:, 0:512] = xT[b][:, oA:oA + 512]
        xkvA_[:, 512:512 + oA] = xT[b][:, 0:oA]
        xkvB_ = np.zeros((E, LB), np.float32)
        xkvB_[:, 0:512] = xT[b][:, oB:oB + 512]
        xkvB_[:, 512:512 + oB] = xT[b][:, 0:oB]
        xq_ = np.ascontiguousarray(
            np.concatenate([xT[b][:, oA:oA + 512], xT[b][:, oB:oB + 512]], axis=1))
        biasA_ = np.full((128, NPA), -CEXP, np.float32)
        biasA_[:, klA // 256:] = NEG
        biasB_ = np.full((128, NPB), -CEXP, np.float32)
        biasB_[:, klB // 256:] = NEG
        m = dict(shared)
        m.update(xkvA=xkvA_, xkvB=xkvB_, xq=xq_, biasA=biasA_, biasB=biasB_)
        in_maps.append(m)
    return in_maps


def _assemble(results):
    outf = np.zeros((B, S, E), np.float32)
    for core in range(8):
        b, c = core // 4, core % 4
        oA, oB = 512 * c, 512 * (7 - c)
        sh = np.asarray(results[core]["out"], np.float32).reshape(E, T)
        outf[b, oA:oA + 512, :] = sh[:, 0:512].T
        outf[b, oB:oB + 512, :] = sh[:, 512:1024].T
    return outf


def kernel(**inputs):
    in_maps = _prep_host(inputs)
    nc = build_nc(debug=False)
    res = run_bass_kernel_spmd(nc, in_maps, core_ids=list(range(8)))
    return _assemble(res.results)
